# version 04
# baseline (speedup 1.0000x reference)
import math
import sys
import numpy as np
import ml_dtypes
import concourse.bass as bass
import concourse.mybir as mybir
from concourse.bass_utils import run_bass_kernel_spmd
from concourse.alu_op_type import AluOpType as AL

F32 = mybir.dt.float32
BF16 = mybir.dt.bfloat16
U8 = mybir.dt.uint8
U32 = mybir.dt.uint32
AF = mybir.ActivationFunctionType
AX = mybir.AxisListType
BF = ml_dtypes.bfloat16

S = 2048
D = 4096
HKV, G, HD = 8, 4, 128
H = HKV * G
INTER = 11008
NT = 1024
NEG_BIG = -3.0e38
SPANS = [512, 512, 1024, 1024, 1536, 1536, 2048, 2048]
LNEPS = 1e-5
RMSEPS = 1e-6

_DBG = None
_SKIP = set()


def _alibi_slopes(n_heads):
    start = 2 ** (-(2 ** (-(math.log2(n_heads) - 3))))
    return np.array([start * start ** i for i in range(n_heads)], dtype=np.float64)


SLOPES = _alibi_slopes(H).reshape(HKV, G)



class _Buf:
    __slots__ = ("w", "rs")

    def __init__(self):
        self.w = None
        self.rs = []

    def on_read(self, op):
        self.rs.append(op)
        return [self.w] if self.w is not None else []

    def on_write(self, op):
        deps = ([self.w] if self.w is not None else []) + self.rs
        self.w = op
        self.rs = []
        return deps


class _Op:
    __slots__ = ("i", "eng", "emit", "deps", "mark", "dma", "sem", "target", "tag",
                 "fence", "slot", "pre_wait")


class Sched:
    ENGINES = ("pe", "dve", "act", "pool", "sp")

    NSLOT = 8

    def __init__(self):
        self.ops = []
        self.per_eng = {e: [] for e in self.ENGINES}
        self._pending = {}
        self._dma_n = {e: 0 for e in self.ENGINES}
        self._slot_last = {}
        self._comp_last = {}

    def add(self, eng, emit, reads=(), writes=(), dma=False, tag="", fence=False):
        op = _Op()
        op.i = len(self.ops)
        op.eng = eng
        op.emit = emit
        op.dma = dma
        op.mark = False
        op.tag = tag
        op.fence = fence
        op.slot = None
        op.pre_wait = None
        if dma:
            op.slot = self._dma_n[eng] % self.NSLOT
            self._dma_n[eng] += 1
            self._slot_last[(eng, op.slot)] = op
        else:
            self._comp_last[eng] = op
        deps = []
        for b in reads:
            deps += b.on_read(op)
        for b in writes:
            deps += b.on_write(op)
        if eng in self._pending:
            deps += self._pending.pop(eng)
        seen = set()
        op.deps = []
        for d in deps:
            if d is not op and id(d) not in seen:
                seen.add(id(d))
                op.deps.append(d)
        self.ops.append(op)
        self.per_eng[eng].append(op)
        return op

    def barrier(self):
        lasts = list(self._slot_last.values()) + list(self._comp_last.values())
        self._pending = {e: list(lasts) for e in self.ENGINES}

    def finalize(self, nc, stack):
        self.comp_sem = {e: stack.enter_context(nc.semaphore("cs_" + e))
                         for e in self.ENGINES}
        self.dmasem = {(e, k): stack.enter_context(nc.semaphore("dq_%s%d" % (e, k)))
                       for e in self.ENGINES for k in range(self.NSLOT)
                       if self._dma_n[e] > 0}
        for op in self.ops:
            for d in op.deps:
                if d.eng != op.eng or d.dma or d.fence or d.eng != "pe":
                    d.mark = True
        for e in self.ENGINES:
            c_comp = 0
            slot_use = [0] * self.NSLOT
            for op in self.per_eng[e]:
                if op.dma:
                    u = slot_use[op.slot]
                    slot_use[op.slot] = u + 1
                    op.sem = self.dmasem[(e, op.slot)]
                    op.target = 16 * (u + 1)
                    op.pre_wait = 16 * u if u else None
                elif op.mark:
                    c_comp += 1
                    op.target = c_comp
                    op.sem = self.comp_sem[e]

    def emit(self, e_name, eng):
        waited = {}
        for op in self.per_eng[e_name]:
            for d in op.deps:
                if d.eng == op.eng and d.eng == "pe" and not d.dma and not d.fence:
                    continue
                key = id(d.sem)
                if waited.get(key, 0) < d.target:
                    eng.wait_ge(d.sem, d.target)
                    waited[key] = d.target
            if op.dma and op.pre_wait is not None:
                key = id(op.sem)
                if waited.get(key, 0) < op.pre_wait:
                    eng.wait_ge(op.sem, op.pre_wait)
                    waited[key] = op.pre_wait
            inst = op.emit(eng) if op.emit is not None else None
            if op.dma:
                inst.then_inc(op.sem, 16)
            elif op.mark:
                assert inst is not None, op.tag
                inst.then_inc(op.sem, 1)



def _lowbyte(a):
    a = np.ascontiguousarray(a.astype(np.int32, copy=False))
    assert sys.byteorder == "little"
    return a.view(np.uint8).reshape(a.shape[0], a.shape[1], 4)[:, :, 0]


def _host_prep(inp):
    sh = {}
    sh["wqT"] = np.ascontiguousarray(np.asarray(inp["L0_attn_Wq"], np.float32).T)
    sh["wkT"] = np.ascontiguousarray(np.asarray(inp["L0_attn_Wk"], np.float32).T)
    sh["wvT"] = np.ascontiguousarray(np.asarray(inp["L0_attn_Wv"], np.float32).T)
    sh["woT"] = np.ascontiguousarray(np.asarray(inp["L0_attn_Wo"], np.float32).T)
    sh["qnw"] = np.ascontiguousarray(
        np.asarray(inp["L0_attn_q_norm_w"], np.float32).reshape(HD, 1))
    sh["knw"] = np.ascontiguousarray(
        np.asarray(inp["L0_attn_k_norm_w"], np.float32).reshape(HD, 1))

    perm = np.concatenate([np.arange(0, INTER, 2), np.arange(1, INTER, 2)])
    gp = _lowbyte(np.asarray(inp["L0_ffn_gate_Wq"]))
    up = _lowbyte(np.asarray(inp["L0_ffn_up_Wq"]))
    dp = _lowbyte(np.asarray(inp["L0_ffn_down_Wq"]))
    sh["gpack"] = np.ascontiguousarray(gp.T[:, perm])
    sh["upack"] = np.ascontiguousarray(up.T[:, perm])
    sh["dpack"] = np.ascontiguousarray(dp.T)

    sh["gsc"] = np.ascontiguousarray(
        np.asarray(inp["L0_ffn_gate_scale"], np.float32).T[:, perm])
    sh["usc"] = np.ascontiguousarray(
        np.asarray(inp["L0_ffn_up_scale"], np.float32).T[:, perm])
    sh["gzr"] = np.ascontiguousarray(
        np.asarray(inp["L0_ffn_gate_zero"], np.float32).T[:, perm])
    sh["uzr"] = np.ascontiguousarray(
        np.asarray(inp["L0_ffn_up_zero"], np.float32).T[:, perm])
    dsT = np.ascontiguousarray(np.asarray(inp["L0_ffn_down_scale"], np.float32).T)
    dzT = np.ascontiguousarray(np.asarray(inp["L0_ffn_down_zero"], np.float32).T)
    sh["dsc0"] = np.ascontiguousarray(dsT[:128])
    sh["dsc1"] = np.ascontiguousarray(np.pad(dsT[128:], ((0, 20), (0, 0))))
    d2s = np.pad(np.concatenate([dsT, dsT], axis=0), ((0, 40), (0, 0)))
    d2z = np.pad(np.concatenate([dzT, dzT], axis=0), ((0, 40), (0, 0)))
    sh["d2s"] = np.ascontiguousarray(
        d2s.reshape(3, 128, D).transpose(1, 0, 2).reshape(128, 3 * D))
    sh["d2z"] = np.ascontiguousarray(
        d2z.reshape(3, 128, D).transpose(1, 0, 2).reshape(128, 3 * D))

    strip_g = np.zeros((128, 5504), dtype=BF)
    for g in range(128):
        strip_g[g, 32 * g:32 * g + 32] = 1
    sh["stripg"] = strip_g
    strip252 = np.zeros((128, 256), dtype=BF)
    for i in range(128):
        strip252[i, 124 + i // 32] = 1
    sh["strip252"] = strip252

    x = np.asarray(inp["L0_attn_x"], dtype=np.float32)
    cores = []
    for c in range(8):
        b = c // 2
        chunks = [0, 3, 4, 7] if c % 2 == 0 else [1, 2, 5, 6]
        rows = np.concatenate([np.arange(ch * 256, ch * 256 + 256) for ch in chunks])
        pc = {
            "xfull": np.ascontiguousarray(x[b]),
            "xown": np.ascontiguousarray(x[b][rows]),
        }
        mask = np.empty((128, 8, 2048), dtype=np.float32)
        tkv = np.arange(2048, dtype=np.float32)
        for t in range(8):
            q0 = chunks[t // 2] * 256 + (t % 2) * 128
            qv = q0 + np.arange(128, dtype=np.float32)
            rel = tkv[None, :] - qv[:, None]
            mask[:, t, :] = np.where(rel > 0, NEG_BIG, rel)
        pc["relmask"] = np.ascontiguousarray(mask.reshape(128, 8 * 2048))
        cores.append((pc, b, chunks))
    return sh, cores



def _build(dbg=None):
    nc = bass.Bass("TRN2", target_bir_lowering=False, detect_race_conditions=False)
    sc = Sched()
    from contextlib import ExitStack
    gstk = ExitStack()

    def din(name, shape, dt):
        return nc.dram_tensor(name, shape, dt, kind="ExternalInput").ap()

    xfull = din("xfull", [S, D], F32)
    xown = din("xown", [NT, D], F32)
    relmask = din("relmask", [128, 8 * 2048], F32)
    wqT = din("wqT", [D, H * HD], F32)
    wkT = din("wkT", [D, HKV * HD], F32)
    wvT = din("wvT", [D, HKV * HD], F32)
    woT = din("woT", [H * HD, D], F32)
    qnw = din("qnw", [HD, 1], F32)
    knw = din("knw", [HD, 1], F32)
    gpack = din("gpack", [2048, INTER], U8)
    upack = din("upack", [2048, INTER], U8)
    dpack = din("dpack", [5504, D], U8)
    gsc = din("gsc", [64, INTER], F32)
    usc = din("usc", [64, INTER], F32)
    gzr = din("gzr", [64, INTER], F32)
    uzr = din("uzr", [64, INTER], F32)
    dsc0 = din("dsc0", [128, D], F32)
    dsc1 = din("dsc1", [64, D], F32)
    d2s = din("d2s", [128, 3 * D], F32)
    d2z = din("d2z", [128, 3 * D], F32)
    stripg = din("stripg", [128, 5504], BF16)
    strip252 = din("strip252", [128, 256], BF16)

    out = nc.dram_tensor("out", [NT, D], F32, kind="ExternalOutput").ap()
    x2d_kind = "ExternalOutput" if dbg == "oproj" else "Internal"
    x2d = nc.dram_tensor("x2d", [NT, D], F32, kind=x2d_kind).ap()
    had_kind = "ExternalOutput" if dbg == "gate" else "Internal"
    had = nc.dram_tensor("had", [INTER, NT], BF16, kind=had_kind).ap()

    dbg_outs = {}

    def dbg_out(name, shape, dt):
        ap = nc.dram_tensor(name, shape, dt, kind="ExternalOutput").ap()
        dbg_outs[name] = ap
        return ap

    def sbin(stk, name, shape, dt, **kw):
        return stk.enter_context(nc.sbuf_tensor(name, shape, dt, **kw))

    pb = [gstk.enter_context(nc.psum_tensor("pb%d" % i, [128, 512], F32))
          for i in range(8)]
    pbB = [_Buf() for _ in range(8)]

    stat = [sbin(gstk, "stat%d" % i, [128, 16], F32) for i in range(2)]
    statB = [_Buf() for _ in range(2)]
    w_comb = sbin(gstk, "w_comb", [128, 1], F32)
    w_combB = _Buf()
    qn_sb = sbin(gstk, "qn_sb", [128, 1], F32)
    kn_sb = sbin(gstk, "kn_sb", [128, 1], F32)
    qnB, knB = _Buf(), _Buf()
    x2p = [sbin(gstk, "x2p%d" % i, [128, 512], F32) for i in range(2)]
    x2pB = [_Buf() for _ in range(2)]
    XgT = sbin(gstk, "XgT", [128, 8, 128], BF16, is_transpose_dest=True)
    XgTB = _Buf()
    rstdq = sbin(gstk, "rstdq", [128, 32], F32)
    rstdqB = _Buf()

    def dma(eng, dst_ap, src_ap, reads, writes, tag="", **kw):
        return sc.add(eng,
                      lambda e, d=dst_ap, s=src_ap, kw=kw: e.dma_start(out=d, in_=s, **kw),
                      reads=reads, writes=writes, dma=True, tag=tag)

    def dmaT(dst_ap, src_ap, reads, writes, tag=""):
        return sc.add("sp", lambda e, d=dst_ap, s=src_ap: e.dma_start_transpose(d, s),
                      reads=reads, writes=writes, dma=True, tag=tag)

    def ln_stats(xin, xinB, jk, jkB, st, stB, eps, tag):
        sc.add("act", lambda e, x=xin, s=st, j=jk: e.activation(
            j[:], x[:], AF.Square, accum_out=s[:, 1:2]),
            reads=[xinB], writes=[jkB, stB], tag=tag + "_sq", fence=True)
        sc.add("act", lambda e, x=xin, s=st, j=jk: e.activation(
            j[:], x[:], AF.Copy, accum_out=s[:, 0:1]),
            reads=[xinB], writes=[jkB, stB], tag=tag + "_s")
        sc.add("dve", lambda e, s=st: e.tensor_scalar(
            s[:, 2:3], s[:, 0:1], 1.0 / D, None, AL.mult),
            reads=[stB], writes=[stB], tag=tag + "_m")
        sc.add("dve", lambda e, s=st: e.tensor_scalar(
            s[:, 3:4], s[:, 1:2], 1.0 / D, None, AL.mult),
            reads=[stB], writes=[stB], tag=tag + "_ssn")
        sc.add("dve", lambda e, s=st: e.tensor_mul(s[:, 4:5], s[:, 2:3], s[:, 2:3]),
               reads=[stB], writes=[stB], tag=tag + "_m2")
        sc.add("dve", lambda e, s=st, ep=eps: e.scalar_tensor_tensor(
            s[:, 5:6], s[:, 3:4], ep, s[:, 4:5], AL.add, AL.subtract),
            reads=[stB], writes=[stB], tag=tag + "_var")
        sc.add("act", lambda e, s=st: e.activation(
            s[:, 6:7], s[:, 5:6], AF.Sqrt),
            reads=[stB], writes=[stB], tag=tag + "_sqrt")
        sc.add("dve", lambda e, s=st: e.reciprocal(s[:, 7:8], s[:, 6:7]),
               reads=[stB], writes=[stB], tag=tag + "_rstd", fence=True)

    def close_phase(stk):
        sc.barrier()
        stk.close()

    stA = ExitStack()
    V_tm = sbin(stA, "V_tm", [128, 16, 1024], BF16)
    VB = [_Buf() for _ in range(16)]
    KhT = sbin(stA, "KhT", [128, 8, 2048], BF16, is_transpose_dest=True)
    KhTB = _Buf()

    st1 = ExitStack()
    K_tm = sbin(st1, "K_tm", [128, 16, 1024], BF16)
    KB = [_Buf() for _ in range(16)]
    rstdk = sbin(st1, "rstdk", [128, 16, 8], F32)
    rstdkB = [_Buf() for _ in range(16)]
    xb1 = sbin(st1, "xb1", [128, 4096], F32)
    xb1B = _Buf()
    y1b1 = sbin(st1, "y1b1", [128, 4096], BF16)
    y1b1B = _Buf()
    jk1 = sbin(st1, "jk1", [128, 4096], BF16)
    jk1B = _Buf()
    y1T5 = sbin(st1, "y1T5", [128, 32, 512], BF16, is_transpose_dest=True)
    y1T5B = _Buf()
    wst1 = [sbin(st1, "wst1_%d" % i, [128, 32, 256], BF16) for i in range(2)]
    wst1B = [_Buf() for _ in range(2)]
    ksq = sbin(st1, "ksq", [128, 1024], BF16)
    ksqB = _Buf()

    def ln_tile(src_dram, t0, xb_, xbB_, y1b_, y1bB_, jk_, jkB_, st_, stB_, tag):
        dma("sp", xb_[:], src_dram[t0:t0 + 128, :], reads=[], writes=[xbB_], tag=tag + "_x")
        ln_stats(xb_, xbB_, jk_, jkB_, st_, stB_, LNEPS, tag)
        sc.add("dve", lambda e, x=xb_, y=y1b_, s=st_: e.tensor_scalar(
            y[:], x[:], s[:, 2:3], s[:, 7:8], AL.subtract, AL.mult),
            reads=[xbB_, stB_], writes=[y1bB_], tag=tag + "_app")

    for tc in range(4):
        for tt in range(4):
            ln_tile(xfull, tc * 512 + tt * 128, xb1, xb1B, y1b1, y1b1B, jk1, jk1B,
                    stat[tt % 2], statB[tt % 2], "a1a_c%d_%d" % (tc, tt))
            dst = bass.AP(y1T5, tt * 128, [[32 * 512, 128], [512, 32], [1, 128]])
            dmaT(dst, y1b1[:], reads=[y1b1B], writes=[y1T5B], tag="a1a_T")
        for fc in range(8):
            is_k = fc < 4
            wsrc = wkT if is_k else wvT
            f0 = (fc % 4) * 256
            wi = fc % 2
            src2 = wsrc.rearrange("(c p) f -> p c f", p=128)[:, :, f0:f0 + 256]
            dma("pool", wst1[wi][:], src2, reads=[], writes=[wst1B[wi]], tag="a1a_w")
            for tt in range(4):
                bank = (fc * 4 + tt) % 3
                ti = tc * 4 + tt
                for dc in range(32):
                    sc.add("pe", lambda e, dc=dc, tt=tt, wi=wi, bank=bank:
                           e.matmul(pb[bank][:, 0:256],
                                    y1T5[:, dc, tt * 128:tt * 128 + 128],
                                    wst1[wi][:, dc, :], start=(dc == 0),
                                    stop=(dc == 31), skip_group_check=True),
                           reads=[y1T5B, wst1B[wi]],
                           writes=[pbB[bank]] if dc in (0, 31) else [],
                           tag="a1a_mm")
                if is_k:
                    sc.add("dve", lambda e, ti=ti, f0=f0, bank=bank: e.tensor_copy(
                        K_tm[:, ti, f0:f0 + 256], pb[bank][:, 0:256]),
                        reads=[pbB[bank]], writes=[KB[ti]], tag="a1a_evK")
                else:
                    sc.add("dve", lambda e, ti=ti, f0=f0, bank=bank: e.tensor_copy(
                        V_tm[:, ti, f0:f0 + 256], pb[bank][:, 0:256]),
                        reads=[pbB[bank]], writes=[VB[ti]], tag="a1a_evV")
    dma("sp", qn_sb[:], qnw[:, :], reads=[], writes=[qnB], tag="qnw")
    dma("sp", kn_sb[:], knw[:, :], reads=[], writes=[knB], tag="knw")
    sc.add("dve", lambda e: e.tensor_mul(w_comb[:], qn_sb[:], kn_sb[:]),
           reads=[qnB, knB], writes=[w_combB], tag="wcomb")
    for ti in range(16):
        sc.add("act", lambda e, ti=ti: e.activation(
            ksq[:], K_tm[:, ti, :], AF.Square),
            reads=[KB[ti]], writes=[ksqB], tag="ksq")
        sc.add("dve", lambda e: e.reduce_sum(
            stat[0][:, 0:8].unsqueeze(2),
            ksq[:].rearrange("p (g s) -> p g s", g=8), axis=AX.X),
            reads=[ksqB], writes=[statB[0]], tag="kss", fence=True)
        sc.add("dve", lambda e: e.tensor_scalar(
            stat[0][:, 8:16], stat[0][:, 0:8], 1.0 / HD, RMSEPS, AL.mult, AL.add),
            reads=[statB[0]], writes=[statB[0]], tag="kssn")
        sc.add("act", lambda e: e.activation(
            stat[0][:, 0:8], stat[0][:, 8:16], AF.Sqrt),
            reads=[statB[0]], writes=[statB[0]], tag="ksqrt")
        sc.add("dve", lambda e, ti=ti: e.reciprocal(rstdk[:, ti, :], stat[0][:, 0:8]),
               reads=[statB[0]], writes=[rstdkB[ti]], tag="krec", fence=True)
        for kv in range(8):
            sc.add("dve", lambda e, ti=ti, kv=kv: e.tensor_scalar(
                K_tm[:, ti, kv * 128:kv * 128 + 128],
                K_tm[:, ti, kv * 128:kv * 128 + 128],
                rstdk[:, ti, kv:kv + 1], None, AL.mult),
                reads=[rstdkB[ti]], writes=[KB[ti]], tag="kapp")
        dst = bass.AP(KhT, ti * 128, [[8 * 2048, 128], [2048, 8], [1, 128]])
        dmaT(dst, K_tm[:, ti, :], reads=[KB[ti]], writes=[KhTB], tag="ktr")
    sc.add("dve", lambda e: e.tensor_scalar(
        KhT[:].rearrange("p a b -> p (a b)"), KhT[:].rearrange("p a b -> p (a b)"),
        w_comb[:, 0:1], None, AL.mult),
        reads=[KhTB, w_combB], writes=[KhTB], tag="kwc")

    if dbg == "a1a":
        dK = dbg_out("dKhT", [128, 8 * 2048], BF16)
        dV = dbg_out("dV", [128, 16 * 1024], BF16)
        dR = dbg_out("dRstdk", [128, 16 * 8], F32)
        dKt = dbg_out("dKtm", [128, 16 * 1024], BF16)
        dma("sp", dK[:, :], KhT[:].rearrange("p a b -> p (a b)"), reads=[KhTB],
            writes=[], tag="dbgK")
        dma("sp", dV[:, :], V_tm[:].rearrange("p a b -> p (a b)"), reads=VB,
            writes=[], tag="dbgV")
        dma("sp", dR[:, :], rstdk[:].rearrange("p a b -> p (a b)"), reads=rstdkB,
            writes=[], tag="dbgR")
        dma("sp", dKt[:, :], K_tm[:].rearrange("p a b -> p (a b)"), reads=KB,
            writes=[], tag="dbgKt")
        dSq = dbg_out("dKsq", [128, 1024], BF16)
        dSt = dbg_out("dStat", [128, 16], F32)
        dma("sp", dSq[:, :], ksq[:], reads=[ksqB], writes=[], tag="dbgSq")
        dma("sp", dSt[:, :], stat[0][:], reads=[statB[0]], writes=[], tag="dbgSt")
        return _emit(nc, sc, gstk, dbg_outs)
    close_phase(st1)

    stAH = ExitStack()
    outT = sbin(stAH, "outT", [128, 32, 512], BF16)
    outTB = _Buf()
    QhT = sbin(stAH, "QhT", [128, 32, 256], BF16, is_transpose_dest=True)
    QhTB = _Buf()

    def phase_q_pair(half, pr):
        gt0 = half * 4 + pr * 2
        stQ = ExitStack()
        y1Tp = sbin(stQ, "y1Tp_%d" % gt0, [128, 32, 256], BF16,
                    is_transpose_dest=True)
        y1TpB = _Buf()
        Q_tm = [sbin(stQ, "Qtm%d_%d" % (gt0, i), [128, 4096], BF16)
                for i in range(2)]
        Q_tmB = [_Buf() for _ in range(2)]
        stLN = ExitStack()
        xbq = sbin(stLN, "xbq_%d" % gt0, [128, 4096], F32)
        xbqB = _Buf()
        y1bq = sbin(stLN, "y1bq_%d" % gt0, [128, 4096], BF16)
        y1bqB = _Buf()
        for t2 in range(2):
            dma("sp", xbq[:], xown[(gt0 + t2) * 128:(gt0 + t2) * 128 + 128, :],
                reads=[], writes=[xbqB], tag="a1b_x")
            ln_stats(xbq, xbqB, y1bq, y1bqB, stat[0], statB[0], LNEPS,
                     "a1b_%d" % (gt0 + t2))
            sc.add("dve", lambda e: e.tensor_scalar(
                y1bq[:], xbq[:], stat[0][:, 2:3], stat[0][:, 7:8],
                AL.subtract, AL.mult),
                reads=[xbqB, statB[0]], writes=[y1bqB], tag="a1b_app")
            dst = bass.AP(y1Tp, t2 * 128, [[32 * 256, 128], [256, 32], [1, 128]])
            dmaT(dst, y1bq[:], reads=[y1bqB], writes=[y1TpB], tag="a1b_T")
        close_phase(stLN)
        stW = ExitStack()
        wstq = [sbin(stW, "wstq%d_%d" % (gt0, i), [128, 32, 256], BF16)
                for i in range(2)]
        wstqB = [_Buf() for _ in range(2)]
        for fc in range(16):
            wi = fc % 2
            src2 = wqT.rearrange("(c p) f -> p c f", p=128)[:, :, fc * 256:fc * 256 + 256]
            dma("pool", wstq[wi][:], src2, reads=[], writes=[wstqB[wi]], tag="a1b_w")
            for t2 in range(2):
                bank = (fc * 2 + t2) % 3
                for dc in range(32):
                    sc.add("pe", lambda e, dc=dc, t2=t2, wi=wi, bank=bank,
                           wstq=wstq, y1Tp=y1Tp: e.matmul(
                               pb[bank][:, 0:256],
                               y1Tp[:, dc, t2 * 128:t2 * 128 + 128],
                               wstq[wi][:, dc, :],
                               start=(dc == 0), stop=(dc == 31),
                               skip_group_check=True),
                           reads=[y1TpB, wstqB[wi]],
                           writes=[pbB[bank]] if dc in (0, 31) else [],
                           tag="a1b_mm")
                sc.add("dve", lambda e, fc=fc, bank=bank, t2=t2, Q_tm=Q_tm:
                       e.tensor_copy(Q_tm[t2][:, fc * 256:fc * 256 + 256],
                                     pb[bank][:, 0:256]),
                       reads=[pbB[bank]], writes=[Q_tmB[t2]], tag="a1b_ev")
        close_phase(stW)
        stR = ExitStack()
        qsq = sbin(stR, "qsq_%d" % gt0, [128, 4096], BF16)
        qsqB = _Buf()
        for t2 in range(2):
            sc.add("act", lambda e, t2=t2, Q_tm=Q_tm, qsq=qsq: e.activation(
                qsq[:], Q_tm[t2][:], AF.Square),
                reads=[Q_tmB[t2]], writes=[qsqB], tag="qsq")
            for hb in range(2):
                sc.add("dve", lambda e, hb=hb, qsq=qsq: e.reduce_sum(
                    stat[hb][:, 0:16].unsqueeze(2),
                    qsq[:, hb * 2048:hb * 2048 + 2048].rearrange(
                        "p (g s) -> p g s", g=16), axis=AX.X),
                    reads=[qsqB], writes=[statB[hb]], tag="qss", fence=True)
                sc.add("dve", lambda e, hb=hb: e.tensor_scalar(
                    stat[hb][:, 0:16], stat[hb][:, 0:16], 1.0, HD * RMSEPS,
                    AL.mult, AL.add), reads=[statB[hb]], writes=[statB[hb]],
                    tag="qeps")
                sc.add("act", lambda e, hb=hb: e.activation(
                    stat[hb][:, 0:16], stat[hb][:, 0:16], AF.Sqrt),
                    reads=[statB[hb]], writes=[statB[hb]], tag="qsqrt")
                sc.add("dve", lambda e, hb=hb: e.reciprocal(
                    rstdq[:, hb * 16:hb * 16 + 16], stat[hb][:, 0:16]),
                    reads=[statB[hb]], writes=[rstdqB], tag="qrec", fence=True)
            for h in range(32):
                sc.add("dve", lambda e, h=h, t2=t2, Q_tm=Q_tm: e.tensor_scalar(
                    Q_tm[t2][:, h * 128:h * 128 + 128],
                    Q_tm[t2][:, h * 128:h * 128 + 128],
                    rstdq[:, h:h + 1], None, AL.mult),
                    reads=[rstdqB], writes=[Q_tmB[t2]], tag="qapp")
            dst = bass.AP(QhT, t2 * 128, [[32 * 256, 128], [256, 32], [1, 128]])
            dmaT(dst, Q_tm[t2][:], reads=[Q_tmB[t2]], writes=[QhTB], tag="qtr")
        close_phase(stR)
        stQ.close()

    def phase_attn_tile(half, qt, qs):
        gt = half * 4 + qt
        span = SPANS[gt]
        nck = span // 512
        ntk = span // 128
        stT = ExitStack()
        ssb = [sbin(stT, "ssb%d_%d" % (gt, i), [128, 2048], F32) for i in range(3)]
        ssbB = [_Buf() for _ in range(3)]
        Ab = [sbin(stT, "Ab%d_%d" % (gt, i), [128, 2048], BF16) for i in range(3)]
        AbB = [_Buf() for _ in range(3)]
        ATb = [sbin(stT, "ATb%d_%d" % (gt, i), [128, 16, 128], BF16,
                    is_transpose_dest=True) for i in range(3)]
        ATbB = [_Buf() for _ in range(3)]
        mkt = sbin(stT, "mk_%d" % gt, [128, 2048], F32)
        mktB = _Buf()
        rsum = [sbin(stT, "rsum%d_%d" % (gt, i), [128, 4], F32) for i in range(3)]
        rsumB = [_Buf() for _ in range(3)]
        dma("sp", mkt[:, 0:span], relmask[:, gt * 2048:gt * 2048 + span],
            reads=[], writes=[mktB], tag="mask")

        def scores_part(idx):
            kv, qh = idx // 4, idx % 4
            h = kv * 4 + qh
            pi = idx % 3
            slope = float(SLOPES[kv, qh])
            for ck in range(nck):
                bank = (idx * nck + ck) % 3
                sc.add("pe", lambda e, h=h, kv=kv, ck=ck, bank=bank: e.matmul(
                    pb[bank][:], QhT[:, h, qs * 128:qs * 128 + 128],
                    KhT[:, kv, ck * 512:ck * 512 + 512],
                    start=True, stop=True, skip_group_check=True),
                    reads=[QhTB, KhTB], writes=[pbB[bank]], tag="sc_mm")
                sc.add("dve", lambda e, pi=pi, ck=ck, slope=slope, bank=bank:
                       e.scalar_tensor_tensor(
                           ssb[pi][:, ck * 512:ck * 512 + 512],
                           mkt[:, ck * 512:ck * 512 + 512], slope,
                           pb[bank][:], AL.mult, AL.add),
                       reads=[mktB, pbB[bank]], writes=[ssbB[pi]], tag="sc_ep")
            sc.add("dve", lambda e, pi=pi: e.reduce_max(
                rsum[pi][:, 0:1].unsqueeze(2),
                ssb[pi][:, 0:span].unsqueeze(1), axis=AX.X, negate=True),
                reads=[ssbB[pi]], writes=[rsumB[pi]], tag="sc_max", fence=True)
            sc.add("act", lambda e, pi=pi: e.activation(
                Ab[pi][:, 0:span], ssb[pi][:, 0:span], AF.Exp,
                bias=rsum[pi][:, 0:1], accum_out=rsum[pi][:, 1:2]),
                reads=[ssbB[pi], rsumB[pi]], writes=[AbB[pi], rsumB[pi]],
                tag="sc_exp")
            sc.add("dve", lambda e, pi=pi: e.reciprocal(
                rsum[pi][:, 2:3], rsum[pi][:, 1:2]),
                reads=[rsumB[pi]], writes=[rsumB[pi]], tag="sc_rec", fence=True)
            rec_bc = bass.AP(rsum[pi], 2, [[4, 128], [0, span]])
            sc.add("pool", lambda e, pi=pi, rec_bc=rec_bc: e.tensor_mul(
                Ab[pi][:, 0:span], Ab[pi][:, 0:span], rec_bc),
                reads=[rsumB[pi]], writes=[AbB[pi]], tag="sc_nrm")
            dst = bass.AP(ATb[pi], 0, [[16 * 128, 128], [128, ntk], [1, 128]])
            dmaT(dst, Ab[pi][:, 0:span], reads=[AbB[pi]], writes=[ATbB[pi]],
                 tag="sc_T")

        def av_part(idx):
            kv, qh = idx // 4, idx % 4
            h = kv * 4 + qh
            pi = idx % 3
            avb = 4 + pi
            for tkc in range(ntk):
                sc.add("pe", lambda e, tkc=tkc, kv=kv, pi=pi, avb=avb: e.matmul(
                    pb[avb][:, 0:128],
                    V_tm[:, tkc, kv * 128:kv * 128 + 128],
                    ATb[pi][:, tkc, :], start=(tkc == 0),
                    stop=(tkc == ntk - 1), skip_group_check=True),
                    reads=[ATbB[pi], VB[tkc]],
                    writes=[pbB[avb]] if tkc in (0, ntk - 1) else [],
                    tag="av_mm")
            sc.add("act", lambda e, h=h, avb=avb: e.activation(
                outT[:, h, qt * 128:qt * 128 + 128], pb[avb][:, 0:128], AF.Copy),
                reads=[pbB[avb]], writes=[outTB], tag="av_ev")

        for idx in range(33):
            if idx < 32:
                scores_part(idx)
            if idx > 0:
                av_part(idx - 1)
        close_phase(stT)

    def phase_oproj(half):
        stOP = ExitStack()
        wsto = [sbin(stOP, "wsto%d_%d" % (half, i), [128, 32, 256], BF16)
                for i in range(2)]
        wstoB = [_Buf() for _ in range(2)]
        for mc in range(16):
            wi = mc % 2
            src = woT.rearrange("(c p) m -> p c m", p=128)[:, :, mc * 256:mc * 256 + 256]
            dma("pool", wsto[wi][:], src, reads=[], writes=[wstoB[wi]], tag="wo")
            for qt in range(4):
                gt = half * 4 + qt
                bank = 6 + (qt % 2)
                for h in range(32):
                    sc.add("pe", lambda e, h=h, qt=qt, wi=wi, bank=bank, wsto=wsto:
                           e.matmul(
                               pb[bank][:, 0:256], outT[:, h, qt * 128:qt * 128 + 128],
                               wsto[wi][:, h, :], start=(h == 0), stop=(h == 31),
                               skip_group_check=True),
                           reads=[outTB, wstoB[wi]],
                           writes=[pbB[bank]] if h in (0, 31) else [], tag="op_mm")
                xi = qt % 2
                dma("sp", x2p[xi][:, 0:256],
                    xown[gt * 128:gt * 128 + 128, mc * 256:mc * 256 + 256],
                    reads=[], writes=[x2pB[xi]], tag="op_x")
                sc.add("dve", lambda e, xi=xi, bank=bank: e.tensor_add(
                    x2p[xi][:, 256:512], pb[bank][:, 0:256], x2p[xi][:, 0:256]),
                    reads=[pbB[bank], x2pB[xi]], writes=[x2pB[xi]], tag="op_add")
                dma("sp", x2d[gt * 128:gt * 128 + 128, mc * 256:mc * 256 + 256],
                    x2p[xi][:, 256:512], reads=[x2pB[xi]], writes=[], tag="op_sp")
        close_phase(stOP)

    for half in range(2):
        for pr in range(2):
            phase_q_pair(half, pr)
            if dbg == "a1b" and half == 0 and pr == 0:
                dQ = dbg_out("dQhT", [128, 32 * 256], BF16)
                dma("sp", dQ[:, :], QhT[:].rearrange("p a b -> p (a b)"),
                    reads=[QhTB], writes=[], tag="dbgQ")
                return _emit(nc, sc, gstk, dbg_outs)
            for t2 in range(2):
                phase_attn_tile(half, pr * 2 + t2, t2)
        if dbg == "a2" and half == 0:
            dO = dbg_out("doutT", [128, 32 * 512], BF16)
            dma("sp", dO[:, :], outT[:].rearrange("p a b -> p (a b)"),
                reads=[outTB], writes=[], tag="dbgO")
            return _emit(nc, sc, gstk, dbg_outs)
        phase_oproj(half)
    close_phase(stAH)
    close_phase(stA)
    if dbg == "oproj":
        return _emit(nc, sc, gstk, dbg_outs)

    stLN = ExitStack()
    y2Te = sbin(stLN, "y2Te", [128, 16, 1024], BF16, is_transpose_dest=True)
    y2To = sbin(stLN, "y2To", [128, 16, 1024], BF16, is_transpose_dest=True)
    y2TeB, y2ToB = _Buf(), _Buf()
    Xg = sbin(stLN, "Xg", [128, 8, 128], BF16)
    XgB = [_Buf() for _ in range(8)]

    stL2 = ExitStack()
    xb2 = [sbin(stL2, "xb2_%d" % i, [128, 4096], F32) for i in range(2)]
    xb2B = [_Buf() for _ in range(2)]
    jk2 = sbin(stL2, "jk2", [128, 4096], BF16)
    jk2B = _Buf()
    y2e = [sbin(stL2, "y2e%d" % i, [128, 2048], BF16) for i in range(2)]
    y2o = [sbin(stL2, "y2o%d" % i, [128, 2048], BF16) for i in range(2)]
    y2eB = [_Buf() for _ in range(2)]
    y2oB = [_Buf() for _ in range(2)]
    xgs = sbin(stL2, "xgs", [128, 128], F32)
    xgsB = _Buf()

    for gt in range(8):
        i = gt % 2
        dma("sp", xb2[i][:], x2d[gt * 128:gt * 128 + 128, :], reads=[],
            writes=[xb2B[i]], tag="ln2_x")
        ln_stats(xb2[i], xb2B[i], jk2, jk2B, stat[i], statB[i], LNEPS, "ln2_%d" % gt)
        for par, yy, yyB in ((0, y2e[i], y2eB[i]), (1, y2o[i], y2oB[i])):
            sc.add("dve", lambda e, i=i, par=par, yy=yy: e.tensor_scalar(
                yy[:], xb2[i][:, par:4096:2], stat[i][:, 2:3], stat[i][:, 7:8],
                AL.subtract, AL.mult),
                reads=[xb2B[i], statB[i]], writes=[yyB], tag="ln2app")
        sc.add("dve", lambda e, i=i: e.reduce_sum(
            xgs[:, 0:64].unsqueeze(2),
            y2e[i][:].rearrange("p (g s) -> p g s", g=64), axis=AX.X),
            reads=[y2eB[i]], writes=[xgsB], tag="xg_e", fence=True)
        sc.add("dve", lambda e, i=i: e.reduce_sum(
            xgs[:, 64:128].unsqueeze(2),
            y2o[i][:].rearrange("p (g s) -> p g s", g=64), axis=AX.X),
            reads=[y2oB[i]], writes=[xgsB], tag="xg_o", fence=True)
        sc.add("dve", lambda e, gt=gt: e.tensor_add(
            Xg[:, gt, 0:64], xgs[:, 0:64], xgs[:, 64:128]),
            reads=[xgsB], writes=[XgB[gt]], tag="xg_add")
        sc.add("dve", lambda e, gt=gt: e.memset(Xg[:, gt, 64:128], 0.0),
               reads=[], writes=[XgB[gt]], tag="xg_pad")
        dstE = bass.AP(y2Te, gt * 128, [[16 * 1024, 128], [1024, 16], [1, 128]])
        dmaT(dstE, y2e[i][:], reads=[y2eB[i]], writes=[y2TeB], tag="y2eT")
        dstO = bass.AP(y2To, gt * 128, [[16 * 1024, 128], [1024, 16], [1, 128]])
        dmaT(dstO, y2o[i][:], reads=[y2oB[i]], writes=[y2ToB], tag="y2oT")
        dstX = bass.AP(XgT, gt * 128, [[8 * 128, 128], [128, 1], [1, 128]])
        dmaT(dstX, Xg[:, gt, :], reads=[XgB[gt]], writes=[XgTB], tag="xgT")

    if dbg == "ln2":
        dE = dbg_out("dy2Te", [128, 16 * 1024], BF16)
        dO2 = dbg_out("dy2To", [128, 16 * 1024], BF16)
        dX = dbg_out("dXgT", [128, 8 * 128], BF16)
        dma("sp", dE[:, :], y2Te[:].rearrange("p a b -> p (a b)"), reads=[y2TeB],
            writes=[], tag="d1")
        dma("sp", dO2[:, :], y2To[:].rearrange("p a b -> p (a b)"), reads=[y2ToB],
            writes=[], tag="d2")
        dma("sp", dX[:, :], XgT[:].rearrange("p a b -> p (a b)"), reads=[XgTB],
            writes=[], tag="d3")
        return _emit(nc, sc, gstk, dbg_outs)
    close_phase(stL2)

    stG = ExitStack()
    gsc_sb = sbin(stG, "gsc_sb", [64, INTER], BF16)
    usc_sb = sbin(stG, "usc_sb", [64, INTER], BF16)
    gzs_sb = sbin(stG, "gzs_sb", [64, INTER], BF16)
    uzs_sb = sbin(stG, "uzs_sb", [64, INTER], BF16)
    gscB, uscB, gzsB, uzsB = _Buf(), _Buf(), _Buf(), _Buf()
    stripg_sb = sbin(stG, "stripg_sb", [64, 2048], BF16)
    stripgB = _Buf()
    pk = [sbin(stG, "pk%d" % i, [128, 512], U8) for i in range(4)]
    pkB = [_Buf() for _ in range(4)]
    dq = [sbin(stG, "dq%d" % i, [128, 256], BF16) for i in range(4)]
    dqB = [_Buf() for _ in range(4)]
    silu_st = sbin(stG, "silu_st", [128, 2, 1024], BF16)
    silu_stB = _Buf()
    hstage = sbin(stG, "hstage", [128, 2, 1024], BF16)
    hstageB = _Buf()

    dma("pool", gsc_sb[:], gsc[:, :], reads=[], writes=[gscB], tag="gsc")
    dma("pool", usc_sb[:], usc[:, :], reads=[], writes=[uscB], tag="usc")
    dma("sp", stripg_sb[:], stripg[0:64, 0:2048], reads=[], writes=[stripgB],
        tag="strip")
    for zsrc, ssb_, sB_, zdst, zB in ((gzr, gsc_sb, gscB, gzs_sb, gzsB),
                                      (uzr, usc_sb, uscB, uzs_sb, uzsB)):
        dma("pool", zdst[:], zsrc[:, :], reads=[], writes=[zB], tag="zr")
        sc.add("dve", lambda e, s=ssb_, z=zdst: e.tensor_mul(z[:], z[:], s[:]),
               reads=[sB_, zB], writes=[zB], tag="zs_m")
        sc.add("dve", lambda e, z=zdst: e.tensor_scalar(
            z[:], z[:], -1.0, None, AL.mult), reads=[zB], writes=[zB], tag="zs_n")

    def xgt_rhs(th):
        return bass.AP(XgT, th * 4 * 128, [[8 * 128, 64], [128, 4], [1, 128]])

    def gu_pk_load(pack, i0, wc):
        pp = wc % 4
        dma("sp", pk[pp][:, 0:256], pack[wc * 128:wc * 128 + 128, i0:i0 + 256],
            reads=[], writes=[pkB[pp]], tag="gu_pk")

    def gu_bc(scl, sclB, i0, wc):
        bb = 4 + wc % 2
        sc.add("pe", lambda e, wc=wc, scl=scl, i0=i0, bb=bb: e.matmul(
            pb[bb][:, 0:256], stripg_sb[:, wc * 128:wc * 128 + 128],
            scl[:, i0:i0 + 256], start=True, stop=True,
            skip_group_check=True),
            reads=[stripgB, sclB], writes=[pbB[bb]], tag="gu_bc")

    for ib in range(43):
        i0 = ib * 256
        for mat in range(2):
            pack = gpack if mat == 0 else upack
            scl, sclB = (gsc_sb, gscB) if mat == 0 else (usc_sb, uscB)
            zs, zsB = (gzs_sb, gzsB) if mat == 0 else (uzs_sb, uzsB)
            for it in range(2):
                for th in range(2):
                    bank = it * 2 + th
                    sc.add("pe", lambda e, bank=bank, zs=zs, i0=i0, it=it, th=th:
                           e.matmul(pb[bank][:],
                                    zs[:, i0 + it * 128:i0 + it * 128 + 128],
                                    xgt_rhs(th), start=True, stop=False,
                                    skip_group_check=True),
                           reads=[zsB, XgTB], writes=[pbB[bank]], tag="gu_corr")
            gu_pk_load(pack, i0, 0)
            gu_pk_load(pack, i0, 1)
            gu_bc(scl, sclB, i0, 0)
            gu_bc(scl, sclB, i0, 1)
            for wc in range(16):
                pp = wc % 4
                bb = 4 + wc % 2
                hi_t, hiB = dq[0 + wc % 2], dqB[0 + wc % 2]
                lo_t, loB = dq[2 + wc % 2], dqB[2 + wc % 2]
                sc.add("dve", lambda e, pp=pp: e.tensor_scalar(
                    pk[pp][:, 256:512].bitcast(U32), pk[pp][:, 0:256].bitcast(U32),
                    4, 0x0F0F0F0F, AL.logical_shift_right, AL.bitwise_and),
                    reads=[pkB[pp]], writes=[pkB[pp]], tag="gu_hi")
                sc.add("dve", lambda e, pp=pp: e.tensor_scalar(
                    pk[pp][:, 0:256].bitcast(U32), pk[pp][:, 0:256].bitcast(U32),
                    0x0F0F0F0F, None, AL.bitwise_and),
                    reads=[pkB[pp]], writes=[pkB[pp]], tag="gu_lo")
                sc.add("dve", lambda e, pp=pp, hi_t=hi_t, bb=bb: e.scalar_tensor_tensor(
                    hi_t[:], pk[pp][:, 256:512], 1, pb[bb][:, 0:256],
                    AL.mult, AL.mult),
                    reads=[pkB[pp], pbB[bb]], writes=[hiB], tag="gu_dqh")
                sc.add("dve", lambda e, pp=pp, lo_t=lo_t, bb=bb: e.scalar_tensor_tensor(
                    lo_t[:], pk[pp][:, 0:256], 1, pb[bb][:, 0:256],
                    AL.mult, AL.mult),
                    reads=[pkB[pp], pbB[bb]], writes=[loB], tag="gu_dql")
                if wc + 2 < 16:
                    gu_pk_load(pack, i0, wc + 2)
                for nib in range(2):
                    wti, wtiB = (hi_t, hiB) if nib == 0 else (lo_t, loB)
                    ysrc, ysrcB = (y2Te, y2TeB) if nib == 0 else (y2To, y2ToB)
                    for it in range(2):
                        for th in range(2):
                            bank = it * 2 + th
                            last = (wc == 15 and nib == 1)
                            sc.add("pe", lambda e, wti=wti, it=it, th=th, wc=wc,
                                   ysrc=ysrc, bank=bank, last=last: e.matmul(
                                       pb[bank][:], wti[:, it * 128:it * 128 + 128],
                                       ysrc[:, wc, th * 512:th * 512 + 512],
                                       start=False, stop=last, skip_group_check=True),
                                   reads=[wtiB, ysrcB],
                                   writes=[pbB[bank]] if last else [], tag="gu_mm")
                if wc + 2 < 16:
                    gu_bc(scl, sclB, i0, wc + 2)
            for it in range(2):
                for th in range(2):
                    bank = it * 2 + th
                    if mat == 0:
                        sc.add("act", lambda e, it=it, th=th, bank=bank: e.activation(
                            silu_st[:, it, th * 512:th * 512 + 512], pb[bank][:],
                            AF.Silu),
                            reads=[pbB[bank]], writes=[silu_stB], tag="gu_silu")
                    else:
                        sc.add("dve", lambda e, it=it, th=th, bank=bank: e.tensor_mul(
                            hstage[:, it, th * 512:th * 512 + 512],
                            silu_st[:, it, th * 512:th * 512 + 512], pb[bank][:]),
                            reads=[silu_stB, pbB[bank]], writes=[hstageB],
                            tag="gu_hmul")
        for it in range(2):
            dma("sp", had[i0 + it * 128:i0 + it * 128 + 128, :],
                hstage[:, it, :], reads=[hstageB], writes=[], tag="gu_spill")

    if dbg == "gate":
        return _emit(nc, sc, gstk, dbg_outs)
    close_phase(stG)
    close_phase(stLN)

    stD = ExitStack()
    hact = sbin(stD, "hact", [128, 86, 512], BF16)
    hactB = [_Buf() for _ in range(86)]
    strip_sb = sbin(stD, "strip_sb", [128, 5504], BF16)
    stripB = _Buf()
    zsd = sbin(stD, "zsd", [128, 3, D], BF16)
    zsdB = _Buf()
    tmpd = sbin(stD, "tmpd", [128, 3 * D], BF16)
    tmpdB = _Buf()
    dscA = sbin(stD, "dscA", [128, D], BF16)
    dscAB = _Buf()
    dscC = sbin(stD, "dscC", [64, D], BF16)
    dscCB = _Buf()
    st252 = sbin(stD, "st252", [128, 256], BF16)
    st252B = _Buf()
    XgT2 = sbin(stD, "XgT2", [128, 3, 512], BF16)
    XgT2B = _Buf()
    dpk = [sbin(stD, "dpk%d" % i, [128, 1024], U8) for i in range(4)]
    dpkB = [_Buf() for _ in range(4)]
    dqd = [sbin(stD, "dqd%d" % i, [128, 512], BF16) for i in range(4)]
    dqdB = [_Buf() for _ in range(4)]
    sclst = [sbin(stD, "sclst%d" % i, [128, 512], BF16) for i in range(2)]
    sclstB = [_Buf() for _ in range(2)]
    outp = [sbin(stD, "outp%d" % i, [128, 512], F32) for i in range(2)]
    outpB = [_Buf() for _ in range(2)]
    b4tok = [_Buf(), _Buf()]

    dma("pool", dscA[:], dsc0[:, :], reads=[], writes=[dscAB], tag="dscA")
    dma("pool", dscC[:], dsc1[:, :], reads=[], writes=[dscCB], tag="dscC")
    dma("sp", st252[:], strip252[:, :], reads=[], writes=[st252B], tag="st252")
    dma("sp", strip_sb[:], stripg[:, :], reads=[], writes=[stripB], tag="stripd")
    dma("pool", zsd[:].rearrange("p a b -> p (a b)"), d2s[:, :], reads=[],
        writes=[zsdB], tag="d2s")
    dma("pool", tmpd[:], d2z[:, :], reads=[], writes=[tmpdB], tag="d2z")
    sc.add("dve", lambda e: e.tensor_mul(
        zsd[:].rearrange("p a b -> p (a b)"), zsd[:].rearrange("p a b -> p (a b)"),
        tmpd[:]), reads=[tmpdB, zsdB], writes=[zsdB], tag="zsd_m")
    sc.add("dve", lambda e: e.tensor_scalar(
        zsd[:].rearrange("p a b -> p (a b)"), zsd[:].rearrange("p a b -> p (a b)"),
        -1.0, None, AL.mult), reads=[zsdB], writes=[zsdB], tag="zsd_n")

    n_th = 1 if dbg in ("down0", "down1") else 2
    n_hc = 1 if dbg == "down1" else (0 if dbg == "down0" else 8)
    for th in range(n_th):
        for ic in range(86):
            dma("sp", hact[:, ic, :],
                had[ic * 128:ic * 128 + 128, th * 512:th * 512 + 512],
                reads=[], writes=[hactB[ic]], tag="dn_h")
        for ic in range(86):
            bk = 5 + ic // 32
            cp = ic % 32
            first = cp == 0
            last = (ic % 32 == 31) or (ic == 85)
            sc.add("pe", lambda e, ic=ic, bk=bk, cp=cp, first=first, last=last:
                   e.matmul(pb[bk][:], st252[:, 124 - 4 * cp:252 - 4 * cp],
                            hact[:, ic, :], start=first, stop=last,
                            skip_group_check=True),
                   reads=[st252B, hactB[ic]],
                   writes=[pbB[bk]] if (first or last) else [], tag="dn_xg")
        for bk2 in range(3):
            sc.add("dve", lambda e, bk2=bk2: e.tensor_copy(
                XgT2[:, bk2, :], pb[5 + bk2][:]),
                reads=[pbB[5 + bk2]], writes=[XgT2B], tag="dn_xge")
        for hc in range(n_hc):
            def dn_pk_load(wc, hc=hc):
                pp = wc % 4
                dma("sp", dpk[pp][:, 0:512],
                    dpack[wc * 128:wc * 128 + 128, hc * 512:hc * 512 + 512],
                    reads=[], writes=[dpkB[pp]], tag="dn_pk")

            def dn_bc(wc, hc=hc):
                bb = 4 + wc % 2
                if wc < 32:
                    sc.add("pe", lambda e, wc=wc, bb=bb: e.matmul(
                        pb[bb][:], strip_sb[0:128, wc * 128:wc * 128 + 128],
                        dscA[:, hc * 512:hc * 512 + 512], start=True, stop=True,
                        skip_group_check=True),
                        reads=[stripB, dscAB], writes=[pbB[bb]], tag="dn_bc")
                else:
                    sc.add("pe", lambda e, wc=wc, bb=bb: e.matmul(
                        pb[bb][:], strip_sb[0:64, (wc - 32) * 128:(wc - 32) * 128 + 128],
                        dscC[:, hc * 512:hc * 512 + 512], start=True, stop=True,
                        skip_group_check=True),
                        reads=[stripB, dscCB], writes=[pbB[bb]], tag="dn_bc")

            dn_pk_load(0)
            dn_pk_load(1)
            dn_bc(0)
            dn_bc(1)
            for wc in range(43):
                pp = wc % 4
                bb = 4 + wc % 2
                b4t = b4tok[wc % 2]
                sc.add("act", lambda e, pp=pp, bb=bb: e.activation(
                    sclst[pp % 2][:], pb[bb][:], AF.Copy),
                    reads=[pbB[bb]], writes=[sclstB[pp % 2], b4t], tag="dn_bcev")
                sc.add("dve", lambda e, pp=pp: e.tensor_scalar(
                    dpk[pp][:, 512:1024].bitcast(U32), dpk[pp][:, 0:512].bitcast(U32),
                    4, 0x0F0F0F0F, AL.logical_shift_right, AL.bitwise_and),
                    reads=[dpkB[pp]], writes=[dpkB[pp]], tag="dn_hi")
                sc.add("dve", lambda e, pp=pp: e.tensor_scalar(
                    dpk[pp][:, 0:512].bitcast(U32), dpk[pp][:, 0:512].bitcast(U32),
                    0x0F0F0F0F, None, AL.bitwise_and),
                    reads=[dpkB[pp]], writes=[dpkB[pp]], tag="dn_lo")
                hi_t, hiB = dqd[0 + wc % 2], dqdB[0 + wc % 2]
                lo_t, loB = dqd[2 + wc % 2], dqdB[2 + wc % 2]
                sc.add("dve", lambda e, pp=pp, hi_t=hi_t, bb=bb: e.scalar_tensor_tensor(
                    hi_t[:], dpk[pp][:, 512:1024], 1, pb[bb][:], AL.mult, AL.mult),
                    reads=[dpkB[pp], pbB[bb], b4t], writes=[hiB], tag="dn_dqh")
                sc.add("pool", lambda e, pp=pp, lo_t=lo_t: e.tensor_mul(
                    lo_t[:], dpk[pp][:, 0:512], sclst[pp % 2][:]),
                    reads=[dpkB[pp], sclstB[pp % 2]], writes=[loB], tag="dn_dql")
                if wc + 2 < 43:
                    dn_pk_load(wc + 2)
                for nib in range(2):
                    wti, wtiB = (hi_t, hiB) if nib == 0 else (lo_t, loB)
                    icc = wc if nib == 0 else 43 + wc
                    for ts in range(4):
                        first = (wc == 0 and nib == 0)
                        sc.add("pe", lambda e, wti=wti, icc=icc, ts=ts, first=first:
                               e.matmul(pb[ts][:],
                                        hact[:, icc, ts * 128:ts * 128 + 128],
                                        wti[:], start=first, stop=False,
                                        skip_group_check=True),
                               reads=[wtiB, hactB[icc]],
                               writes=[pbB[ts]] if first else [], tag="dn_mm")
                if wc + 2 < 43:
                    dn_bc(wc + 2)
            for ts in range(4):
                for b2 in range(3):
                    last = b2 == 2
                    sc.add("pe", lambda e, ts=ts, b2=b2, hc=hc, last=last: e.matmul(
                        pb[ts][:], XgT2[:, b2, ts * 128:ts * 128 + 128],
                        zsd[:, b2, hc * 512:hc * 512 + 512],
                        start=False, stop=last, skip_group_check=True),
                        reads=[XgT2B, zsdB],
                        writes=[pbB[ts]] if last else [], tag="dn_corr")
                oi = ts % 2
                r0 = th * 512 + ts * 128
                dma("sp", x2p[oi][:, 0:512],
                    x2d[r0:r0 + 128, hc * 512:hc * 512 + 512],
                    reads=[], writes=[x2pB[oi]], tag="dn_x2")
                sc.add("dve", lambda e, oi=oi, ts=ts: e.tensor_add(
                    outp[oi][:], pb[ts][:], x2p[oi][:, 0:512]),
                    reads=[pbB[ts], x2pB[oi]], writes=[outpB[oi]], tag="dn_add")
                dma("sp", out[r0:r0 + 128, hc * 512:hc * 512 + 512], outp[oi][:],
                    reads=[outpB[oi]], writes=[], tag="dn_out")

    if dbg in ("down0", "down1"):
        dX2 = dbg_out("dXgT2", [128, 3 * 512], BF16)
        dma("sp", dX2[:, :], XgT2[:].rearrange("p a b -> p (a b)"), reads=[XgT2B],
            writes=[], tag="dx2")
    close_phase(stD)
    return _emit(nc, sc, gstk, dbg_outs)


def _emit(nc, sc, gstk, dbg_outs):
    sc.barrier()
    sc.add("sp", lambda e: e.nop(), reads=[], writes=[], tag="final")
    sc.finalize(nc, gstk)
    with nc.Block() as block:
        @block.tensor
        def _(e):
            sc.emit("pe", e)

        @block.vector
        def _(e):
            sc.emit("dve", e)

        @block.scalar
        def _(e):
            sc.emit("act", e)

        @block.gpsimd
        def _(e):
            sc.emit("pool", e)

        @block.sync
        def _(e):
            sc.emit("sp", e)
    return nc, dbg_outs


def kernel(**inputs):
    sh, cores = _host_prep(inputs)
    nc, dbg_outs = _build(_DBG)
    in_maps = []
    for pc, b, chunks in cores:
        m = dict(sh)
        m.update(pc)
        in_maps.append(m)
    res = run_bass_kernel_spmd(nc, in_maps, core_ids=list(range(8)))
    out = np.zeros((4, 2048, 4096), dtype=np.float32)
    for c, (pc, b, chunks) in enumerate(cores):
        oc = np.asarray(res.results[c]["out"])
        for s_, ch in enumerate(chunks):
            out[b, ch * 256:(ch + 1) * 256] = oc[s_ * 256:(s_ + 1) * 256]
    return out


# version 05
# speedup vs baseline: 1.0689x; 1.0689x over previous
import math
import sys
import numpy as np
import ml_dtypes
import concourse.bass as bass
import concourse.mybir as mybir
from concourse.bass_utils import run_bass_kernel_spmd
from concourse.alu_op_type import AluOpType as AL

F32 = mybir.dt.float32
BF16 = mybir.dt.bfloat16
U8 = mybir.dt.uint8
U32 = mybir.dt.uint32
AF = mybir.ActivationFunctionType
AX = mybir.AxisListType
BF = ml_dtypes.bfloat16

S = 2048
D = 4096
HKV, G, HD = 8, 4, 128
H = HKV * G
INTER = 11008
NT = 1024
NEG_BIG = -3.0e38
SPANS = [512, 512, 1024, 1024, 1536, 1536, 2048, 2048]
LNEPS = 1e-5
RMSEPS = 1e-6

_DBG = None
_SKIP = set()


def _alibi_slopes(n_heads):
    start = 2 ** (-(2 ** (-(math.log2(n_heads) - 3))))
    return np.array([start * start ** i for i in range(n_heads)], dtype=np.float64)


SLOPES = _alibi_slopes(H).reshape(HKV, G)



class _Buf:
    __slots__ = ("w", "rs")

    def __init__(self):
        self.w = None
        self.rs = []

    def on_read(self, op):
        self.rs.append(op)
        return [self.w] if self.w is not None else []

    def on_write(self, op):
        deps = ([self.w] if self.w is not None else []) + self.rs
        self.w = op
        self.rs = []
        return deps


class _Op:
    __slots__ = ("i", "eng", "emit", "deps", "mark", "dma", "sem", "target", "tag",
                 "fence", "slot", "pre_wait")


class Sched:
    ENGINES = ("pe", "dve", "act", "pool", "sp")

    NSLOT = 8

    def __init__(self):
        self.ops = []
        self.per_eng = {e: [] for e in self.ENGINES}
        self._pending = {}
        self._dma_n = {e: 0 for e in self.ENGINES}
        self._slot_last = {}
        self._comp_last = {}

    def add(self, eng, emit, reads=(), writes=(), dma=False, tag="", fence=False):
        op = _Op()
        op.i = len(self.ops)
        op.eng = eng
        op.emit = emit
        op.dma = dma
        op.mark = False
        op.tag = tag
        op.fence = fence
        op.slot = None
        op.pre_wait = None
        if dma:
            op.slot = self._dma_n[eng] % self.NSLOT
            self._dma_n[eng] += 1
            self._slot_last[(eng, op.slot)] = op
        else:
            self._comp_last[eng] = op
        deps = []
        for b in reads:
            deps += b.on_read(op)
        for b in writes:
            deps += b.on_write(op)
        if eng in self._pending:
            deps += self._pending.pop(eng)
        seen = set()
        op.deps = []
        for d in deps:
            if d is not op and id(d) not in seen:
                seen.add(id(d))
                op.deps.append(d)
        self.ops.append(op)
        self.per_eng[eng].append(op)
        return op

    def barrier(self):
        lasts = list(self._slot_last.values()) + list(self._comp_last.values())
        self._pending = {e: list(lasts) for e in self.ENGINES}

    def finalize(self, nc, stack):
        self.comp_sem = {e: stack.enter_context(nc.semaphore("cs_" + e))
                         for e in self.ENGINES}
        self.dmasem = {(e, k): stack.enter_context(nc.semaphore("dq_%s%d" % (e, k)))
                       for e in self.ENGINES for k in range(self.NSLOT)
                       if self._dma_n[e] > 0}
        for op in self.ops:
            for d in op.deps:
                if d.eng != op.eng or d.dma or d.fence or d.eng != "pe":
                    d.mark = True
        for e in self.ENGINES:
            c_comp = 0
            slot_use = [0] * self.NSLOT
            for op in self.per_eng[e]:
                if op.dma:
                    u = slot_use[op.slot]
                    slot_use[op.slot] = u + 1
                    op.sem = self.dmasem[(e, op.slot)]
                    op.target = 16 * (u + 1)
                    op.pre_wait = 16 * u if u else None
                elif op.mark:
                    c_comp += 1
                    op.target = c_comp
                    op.sem = self.comp_sem[e]

    def emit(self, e_name, eng):
        waited = {}
        for op in self.per_eng[e_name]:
            for d in op.deps:
                if d.eng == op.eng and d.eng == "pe" and not d.dma and not d.fence:
                    continue
                key = id(d.sem)
                if waited.get(key, 0) < d.target:
                    eng.wait_ge(d.sem, d.target)
                    waited[key] = d.target
            if op.dma and op.pre_wait is not None:
                key = id(op.sem)
                if waited.get(key, 0) < op.pre_wait:
                    eng.wait_ge(op.sem, op.pre_wait)
                    waited[key] = op.pre_wait
            inst = op.emit(eng) if op.emit is not None else None
            if op.dma:
                inst.then_inc(op.sem, 16)
            elif op.mark:
                assert inst is not None, op.tag
                inst.then_inc(op.sem, 1)



def _lowbyte(a):
    a = np.ascontiguousarray(a.astype(np.int32, copy=False))
    assert sys.byteorder == "little"
    return a.view(np.uint8).reshape(a.shape[0], a.shape[1], 4)[:, :, 0]


def _host_prep(inp):
    sh = {}
    sh["wqT"] = np.ascontiguousarray(np.asarray(inp["L0_attn_Wq"], np.float32).T)
    sh["wkT"] = np.ascontiguousarray(np.asarray(inp["L0_attn_Wk"], np.float32).T)
    sh["wvT"] = np.ascontiguousarray(np.asarray(inp["L0_attn_Wv"], np.float32).T)
    sh["woT"] = np.ascontiguousarray(np.asarray(inp["L0_attn_Wo"], np.float32).T)
    sh["qnw"] = np.ascontiguousarray(
        np.asarray(inp["L0_attn_q_norm_w"], np.float32).reshape(HD, 1))
    sh["knw"] = np.ascontiguousarray(
        np.asarray(inp["L0_attn_k_norm_w"], np.float32).reshape(HD, 1))

    perm = np.concatenate([np.arange(0, INTER, 2), np.arange(1, INTER, 2)])
    gp = _lowbyte(np.asarray(inp["L0_ffn_gate_Wq"]))
    up = _lowbyte(np.asarray(inp["L0_ffn_up_Wq"]))
    dp = _lowbyte(np.asarray(inp["L0_ffn_down_Wq"]))
    sh["gpack"] = np.ascontiguousarray(gp.T[:, perm])
    sh["upack"] = np.ascontiguousarray(up.T[:, perm])
    sh["dpack"] = np.ascontiguousarray(dp.T)

    sh["gsc"] = np.ascontiguousarray(
        np.asarray(inp["L0_ffn_gate_scale"], np.float32).T[:, perm])
    sh["usc"] = np.ascontiguousarray(
        np.asarray(inp["L0_ffn_up_scale"], np.float32).T[:, perm])
    sh["gzr"] = np.ascontiguousarray(
        np.asarray(inp["L0_ffn_gate_zero"], np.float32).T[:, perm])
    sh["uzr"] = np.ascontiguousarray(
        np.asarray(inp["L0_ffn_up_zero"], np.float32).T[:, perm])
    dsT = np.ascontiguousarray(np.asarray(inp["L0_ffn_down_scale"], np.float32).T)
    dzT = np.ascontiguousarray(np.asarray(inp["L0_ffn_down_zero"], np.float32).T)
    sh["dsc0"] = np.ascontiguousarray(dsT[:128])
    sh["dsc1"] = np.ascontiguousarray(np.pad(dsT[128:], ((0, 20), (0, 0))))
    d2s = np.pad(np.concatenate([dsT, dsT], axis=0), ((0, 40), (0, 0)))
    d2z = np.pad(np.concatenate([dzT, dzT], axis=0), ((0, 40), (0, 0)))
    sh["d2s"] = np.ascontiguousarray(
        d2s.reshape(3, 128, D).transpose(1, 0, 2).reshape(128, 3 * D))
    sh["d2z"] = np.ascontiguousarray(
        d2z.reshape(3, 128, D).transpose(1, 0, 2).reshape(128, 3 * D))

    strip_g = np.zeros((128, 5504), dtype=BF)
    for g in range(128):
        strip_g[g, 32 * g:32 * g + 32] = 1
    sh["stripg"] = strip_g
    strip252 = np.zeros((128, 256), dtype=BF)
    for i in range(128):
        strip252[i, 124 + i // 32] = 1
    sh["strip252"] = strip252

    x = np.asarray(inp["L0_attn_x"], dtype=np.float32)
    cores = []
    for c in range(8):
        b = c // 2
        chunks = [0, 3, 4, 7] if c % 2 == 0 else [1, 2, 5, 6]
        rows = np.concatenate([np.arange(ch * 256, ch * 256 + 256) for ch in chunks])
        pc = {
            "xfull": np.ascontiguousarray(x[b]),
            "xown": np.ascontiguousarray(x[b][rows]),
        }
        mask = np.empty((128, 8, 2048), dtype=np.float32)
        tkv = np.arange(2048, dtype=np.float32)
        for t in range(8):
            q0 = chunks[t // 2] * 256 + (t % 2) * 128
            qv = q0 + np.arange(128, dtype=np.float32)
            rel = tkv[None, :] - qv[:, None]
            mask[:, t, :] = np.where(rel > 0, NEG_BIG, rel)
        pc["relmask"] = np.ascontiguousarray(mask.reshape(128, 8 * 2048))
        cores.append((pc, b, chunks))
    return sh, cores



def _build(dbg=None):
    nc = bass.Bass("TRN2", target_bir_lowering=False, detect_race_conditions=False)
    sc = Sched()
    from contextlib import ExitStack
    gstk = ExitStack()

    def din(name, shape, dt):
        return nc.dram_tensor(name, shape, dt, kind="ExternalInput").ap()

    xfull = din("xfull", [S, D], F32)
    xown = din("xown", [NT, D], F32)
    relmask = din("relmask", [128, 8 * 2048], F32)
    wqT = din("wqT", [D, H * HD], F32)
    wkT = din("wkT", [D, HKV * HD], F32)
    wvT = din("wvT", [D, HKV * HD], F32)
    woT = din("woT", [H * HD, D], F32)
    qnw = din("qnw", [HD, 1], F32)
    knw = din("knw", [HD, 1], F32)
    gpack = din("gpack", [2048, INTER], U8)
    upack = din("upack", [2048, INTER], U8)
    dpack = din("dpack", [5504, D], U8)
    gsc = din("gsc", [64, INTER], F32)
    usc = din("usc", [64, INTER], F32)
    gzr = din("gzr", [64, INTER], F32)
    uzr = din("uzr", [64, INTER], F32)
    dsc0 = din("dsc0", [128, D], F32)
    dsc1 = din("dsc1", [64, D], F32)
    d2s = din("d2s", [128, 3 * D], F32)
    d2z = din("d2z", [128, 3 * D], F32)
    stripg = din("stripg", [128, 5504], BF16)
    strip252 = din("strip252", [128, 256], BF16)

    out = nc.dram_tensor("out", [NT, D], F32, kind="ExternalOutput").ap()
    x2d_kind = "ExternalOutput" if dbg == "oproj" else "Internal"
    x2d = nc.dram_tensor("x2d", [NT, D], F32, kind=x2d_kind).ap()
    had_kind = "ExternalOutput" if dbg == "gate" else "Internal"
    had = nc.dram_tensor("had", [INTER, NT], BF16, kind=had_kind).ap()

    dbg_outs = {}

    def dbg_out(name, shape, dt):
        ap = nc.dram_tensor(name, shape, dt, kind="ExternalOutput").ap()
        dbg_outs[name] = ap
        return ap

    def sbin(stk, name, shape, dt, **kw):
        return stk.enter_context(nc.sbuf_tensor(name, shape, dt, **kw))

    pb = [gstk.enter_context(nc.psum_tensor("pb%d" % i, [128, 512], F32))
          for i in range(8)]
    pbB = [_Buf() for _ in range(8)]

    stat = [sbin(gstk, "stat%d" % i, [128, 16], F32) for i in range(2)]
    statB = [_Buf() for _ in range(2)]
    w_comb = sbin(gstk, "w_comb", [128, 1], F32)
    w_combB = _Buf()
    qn_sb = sbin(gstk, "qn_sb", [128, 1], F32)
    kn_sb = sbin(gstk, "kn_sb", [128, 1], F32)
    qnB, knB = _Buf(), _Buf()
    x2p = [sbin(gstk, "x2p%d" % i, [128, 512], F32) for i in range(2)]
    x2pB = [_Buf() for _ in range(2)]
    XgT = sbin(gstk, "XgT", [128, 8, 128], BF16, is_transpose_dest=True)
    XgTB = _Buf()
    rstdq = sbin(gstk, "rstdq", [128, 32], F32)
    rstdqB = _Buf()

    def dma(eng, dst_ap, src_ap, reads, writes, tag="", **kw):
        return sc.add(eng,
                      lambda e, d=dst_ap, s=src_ap, kw=kw: e.dma_start(out=d, in_=s, **kw),
                      reads=reads, writes=writes, dma=True, tag=tag)

    def dmaT(dst_ap, src_ap, reads, writes, tag=""):
        return sc.add("sp", lambda e, d=dst_ap, s=src_ap: e.dma_start_transpose(d, s),
                      reads=reads, writes=writes, dma=True, tag=tag)

    def ln_stats(xin, xinB, jk, jkB, st, stB, eps, tag):
        sc.add("act", lambda e, x=xin, s=st, j=jk: e.activation(
            j[:], x[:], AF.Square, accum_out=s[:, 1:2]),
            reads=[xinB], writes=[jkB, stB], tag=tag + "_sq", fence=True)
        sc.add("act", lambda e, x=xin, s=st, j=jk: e.activation(
            j[:], x[:], AF.Copy, accum_out=s[:, 0:1]),
            reads=[xinB], writes=[jkB, stB], tag=tag + "_s")
        sc.add("dve", lambda e, s=st: e.tensor_scalar(
            s[:, 2:3], s[:, 0:1], 1.0 / D, None, AL.mult),
            reads=[stB], writes=[stB], tag=tag + "_m")
        sc.add("dve", lambda e, s=st: e.tensor_scalar(
            s[:, 3:4], s[:, 1:2], 1.0 / D, None, AL.mult),
            reads=[stB], writes=[stB], tag=tag + "_ssn")
        sc.add("dve", lambda e, s=st: e.tensor_mul(s[:, 4:5], s[:, 2:3], s[:, 2:3]),
               reads=[stB], writes=[stB], tag=tag + "_m2")
        sc.add("dve", lambda e, s=st, ep=eps: e.scalar_tensor_tensor(
            s[:, 5:6], s[:, 3:4], ep, s[:, 4:5], AL.add, AL.subtract),
            reads=[stB], writes=[stB], tag=tag + "_var")
        sc.add("act", lambda e, s=st: e.activation(
            s[:, 6:7], s[:, 5:6], AF.Sqrt),
            reads=[stB], writes=[stB], tag=tag + "_sqrt")
        sc.add("dve", lambda e, s=st: e.reciprocal(s[:, 7:8], s[:, 6:7]),
               reads=[stB], writes=[stB], tag=tag + "_rstd", fence=True)

    def close_phase(stk):
        sc.barrier()
        stk.close()

    stA = ExitStack()
    V_tm = sbin(stA, "V_tm", [128, 16, 1024], BF16)
    VB = [_Buf() for _ in range(16)]
    KhT = sbin(stA, "KhT", [128, 8, 2048], BF16, is_transpose_dest=True)
    KhTB = _Buf()

    st1 = ExitStack()
    K_tm = sbin(st1, "K_tm", [128, 16, 1024], BF16)
    KB = [_Buf() for _ in range(16)]
    rstdk = sbin(st1, "rstdk", [128, 16, 8], F32)
    rstdkB = [_Buf() for _ in range(16)]
    xb1 = sbin(st1, "xb1", [128, 4096], F32)
    xb1B = _Buf()
    y1b1 = sbin(st1, "y1b1", [128, 4096], BF16)
    y1b1B = _Buf()
    jk1 = sbin(st1, "jk1", [128, 4096], BF16)
    jk1B = _Buf()
    y1T5 = sbin(st1, "y1T5", [128, 32, 512], BF16, is_transpose_dest=True)
    y1T5B = _Buf()
    wst1 = [sbin(st1, "wst1_%d" % i, [128, 32, 256], BF16) for i in range(2)]
    wst1B = [_Buf() for _ in range(2)]
    ksq = sbin(st1, "ksq", [128, 1024], BF16)
    ksqB = _Buf()

    def ln_tile(src_dram, t0, xb_, xbB_, y1b_, y1bB_, jk_, jkB_, st_, stB_, tag):
        dma("sp", xb_[:], src_dram[t0:t0 + 128, :], reads=[], writes=[xbB_], tag=tag + "_x")
        ln_stats(xb_, xbB_, jk_, jkB_, st_, stB_, LNEPS, tag)
        sc.add("dve", lambda e, x=xb_, y=y1b_, s=st_: e.tensor_scalar(
            y[:], x[:], s[:, 2:3], s[:, 7:8], AL.subtract, AL.mult),
            reads=[xbB_, stB_], writes=[y1bB_], tag=tag + "_app")

    for tc in range(4):
        for tt in range(4):
            ln_tile(xfull, tc * 512 + tt * 128, xb1, xb1B, y1b1, y1b1B, jk1, jk1B,
                    stat[tt % 2], statB[tt % 2], "a1a_c%d_%d" % (tc, tt))
            dst = bass.AP(y1T5, tt * 128, [[32 * 512, 128], [512, 32], [1, 128]])
            dmaT(dst, y1b1[:], reads=[y1b1B], writes=[y1T5B], tag="a1a_T")
        for fc in range(8):
            is_k = fc < 4
            wsrc = wkT if is_k else wvT
            f0 = (fc % 4) * 256
            wi = fc % 2
            src2 = wsrc.rearrange("(c p) f -> p c f", p=128)[:, :, f0:f0 + 256]
            dma("pool", wst1[wi][:], src2, reads=[], writes=[wst1B[wi]], tag="a1a_w")
            for tt in range(4):
                bank = (fc * 4 + tt) % 3
                ti = tc * 4 + tt
                for dc in range(32):
                    sc.add("pe", lambda e, dc=dc, tt=tt, wi=wi, bank=bank:
                           e.matmul(pb[bank][:, 0:256],
                                    y1T5[:, dc, tt * 128:tt * 128 + 128],
                                    wst1[wi][:, dc, :], start=(dc == 0),
                                    stop=(dc == 31), skip_group_check=True),
                           reads=[y1T5B, wst1B[wi]],
                           writes=[pbB[bank]] if dc in (0, 31) else [],
                           tag="a1a_mm")
                if is_k:
                    sc.add("dve", lambda e, ti=ti, f0=f0, bank=bank: e.tensor_copy(
                        K_tm[:, ti, f0:f0 + 256], pb[bank][:, 0:256]),
                        reads=[pbB[bank]], writes=[KB[ti]], tag="a1a_evK")
                else:
                    sc.add("dve", lambda e, ti=ti, f0=f0, bank=bank: e.tensor_copy(
                        V_tm[:, ti, f0:f0 + 256], pb[bank][:, 0:256]),
                        reads=[pbB[bank]], writes=[VB[ti]], tag="a1a_evV")
    dma("sp", qn_sb[:], qnw[:, :], reads=[], writes=[qnB], tag="qnw")
    dma("sp", kn_sb[:], knw[:, :], reads=[], writes=[knB], tag="knw")
    sc.add("dve", lambda e: e.tensor_mul(w_comb[:], qn_sb[:], kn_sb[:]),
           reads=[qnB, knB], writes=[w_combB], tag="wcomb")
    for ti in range(16):
        sc.add("act", lambda e, ti=ti: e.activation(
            ksq[:], K_tm[:, ti, :], AF.Square),
            reads=[KB[ti]], writes=[ksqB], tag="ksq")
        sc.add("dve", lambda e: e.reduce_sum(
            stat[0][:, 0:8].unsqueeze(2),
            ksq[:].rearrange("p (g s) -> p g s", g=8), axis=AX.X),
            reads=[ksqB], writes=[statB[0]], tag="kss", fence=True)
        sc.add("dve", lambda e: e.tensor_scalar(
            stat[0][:, 8:16], stat[0][:, 0:8], 1.0 / HD, RMSEPS, AL.mult, AL.add),
            reads=[statB[0]], writes=[statB[0]], tag="kssn")
        sc.add("act", lambda e: e.activation(
            stat[0][:, 0:8], stat[0][:, 8:16], AF.Sqrt),
            reads=[statB[0]], writes=[statB[0]], tag="ksqrt")
        sc.add("dve", lambda e, ti=ti: e.reciprocal(rstdk[:, ti, :], stat[0][:, 0:8]),
               reads=[statB[0]], writes=[rstdkB[ti]], tag="krec", fence=True)
        for kv in range(8):
            sc.add("dve", lambda e, ti=ti, kv=kv: e.tensor_scalar(
                K_tm[:, ti, kv * 128:kv * 128 + 128],
                K_tm[:, ti, kv * 128:kv * 128 + 128],
                rstdk[:, ti, kv:kv + 1], None, AL.mult),
                reads=[rstdkB[ti]], writes=[KB[ti]], tag="kapp")
        dst = bass.AP(KhT, ti * 128, [[8 * 2048, 128], [2048, 8], [1, 128]])
        dmaT(dst, K_tm[:, ti, :], reads=[KB[ti]], writes=[KhTB], tag="ktr")
    sc.add("dve", lambda e: e.tensor_scalar(
        KhT[:].rearrange("p a b -> p (a b)"), KhT[:].rearrange("p a b -> p (a b)"),
        w_comb[:, 0:1], None, AL.mult),
        reads=[KhTB, w_combB], writes=[KhTB], tag="kwc")

    if dbg == "a1a":
        dK = dbg_out("dKhT", [128, 8 * 2048], BF16)
        dV = dbg_out("dV", [128, 16 * 1024], BF16)
        dR = dbg_out("dRstdk", [128, 16 * 8], F32)
        dKt = dbg_out("dKtm", [128, 16 * 1024], BF16)
        dma("sp", dK[:, :], KhT[:].rearrange("p a b -> p (a b)"), reads=[KhTB],
            writes=[], tag="dbgK")
        dma("sp", dV[:, :], V_tm[:].rearrange("p a b -> p (a b)"), reads=VB,
            writes=[], tag="dbgV")
        dma("sp", dR[:, :], rstdk[:].rearrange("p a b -> p (a b)"), reads=rstdkB,
            writes=[], tag="dbgR")
        dma("sp", dKt[:, :], K_tm[:].rearrange("p a b -> p (a b)"), reads=KB,
            writes=[], tag="dbgKt")
        dSq = dbg_out("dKsq", [128, 1024], BF16)
        dSt = dbg_out("dStat", [128, 16], F32)
        dma("sp", dSq[:, :], ksq[:], reads=[ksqB], writes=[], tag="dbgSq")
        dma("sp", dSt[:, :], stat[0][:], reads=[statB[0]], writes=[], tag="dbgSt")
        return _emit(nc, sc, gstk, dbg_outs)
    close_phase(st1)

    stAH = ExitStack()
    outT = sbin(stAH, "outT", [128, 32, 512], BF16)
    outTB = _Buf()
    QhT = sbin(stAH, "QhT", [128, 32, 256], BF16, is_transpose_dest=True)
    QhTB = _Buf()

    def phase_q_pair(half, pr):
        gt0 = half * 4 + pr * 2
        stQ = ExitStack()
        y1Tp = sbin(stQ, "y1Tp_%d" % gt0, [128, 32, 256], BF16,
                    is_transpose_dest=True)
        y1TpB = _Buf()
        Q_tm = [sbin(stQ, "Qtm%d_%d" % (gt0, i), [128, 4096], BF16)
                for i in range(2)]
        Q_tmB = [_Buf() for _ in range(2)]
        stLN = ExitStack()
        xbq = sbin(stLN, "xbq_%d" % gt0, [128, 4096], F32)
        xbqB = _Buf()
        y1bq = sbin(stLN, "y1bq_%d" % gt0, [128, 4096], BF16)
        y1bqB = _Buf()
        for t2 in range(2):
            dma("sp", xbq[:], xown[(gt0 + t2) * 128:(gt0 + t2) * 128 + 128, :],
                reads=[], writes=[xbqB], tag="a1b_x")
            ln_stats(xbq, xbqB, y1bq, y1bqB, stat[0], statB[0], LNEPS,
                     "a1b_%d" % (gt0 + t2))
            sc.add("dve", lambda e: e.tensor_scalar(
                y1bq[:], xbq[:], stat[0][:, 2:3], stat[0][:, 7:8],
                AL.subtract, AL.mult),
                reads=[xbqB, statB[0]], writes=[y1bqB], tag="a1b_app")
            dst = bass.AP(y1Tp, t2 * 128, [[32 * 256, 128], [256, 32], [1, 128]])
            dmaT(dst, y1bq[:], reads=[y1bqB], writes=[y1TpB], tag="a1b_T")
        close_phase(stLN)
        stW = ExitStack()
        wstq = [sbin(stW, "wstq%d_%d" % (gt0, i), [128, 32, 256], BF16)
                for i in range(2)]
        wstqB = [_Buf() for _ in range(2)]
        for fc in range(16):
            wi = fc % 2
            src2 = wqT.rearrange("(c p) f -> p c f", p=128)[:, :, fc * 256:fc * 256 + 256]
            dma("pool", wstq[wi][:], src2, reads=[], writes=[wstqB[wi]], tag="a1b_w")
            for t2 in range(2):
                bank = (fc * 2 + t2) % 3
                for dc in range(32):
                    sc.add("pe", lambda e, dc=dc, t2=t2, wi=wi, bank=bank,
                           wstq=wstq, y1Tp=y1Tp: e.matmul(
                               pb[bank][:, 0:256],
                               y1Tp[:, dc, t2 * 128:t2 * 128 + 128],
                               wstq[wi][:, dc, :],
                               start=(dc == 0), stop=(dc == 31),
                               skip_group_check=True),
                           reads=[y1TpB, wstqB[wi]],
                           writes=[pbB[bank]] if dc in (0, 31) else [],
                           tag="a1b_mm")
                sc.add("dve", lambda e, fc=fc, bank=bank, t2=t2, Q_tm=Q_tm:
                       e.tensor_copy(Q_tm[t2][:, fc * 256:fc * 256 + 256],
                                     pb[bank][:, 0:256]),
                       reads=[pbB[bank]], writes=[Q_tmB[t2]], tag="a1b_ev")
        close_phase(stW)
        stR = ExitStack()
        qsq = sbin(stR, "qsq_%d" % gt0, [128, 4096], BF16)
        qsqB = _Buf()
        for t2 in range(2):
            sc.add("act", lambda e, t2=t2, Q_tm=Q_tm, qsq=qsq: e.activation(
                qsq[:], Q_tm[t2][:], AF.Square),
                reads=[Q_tmB[t2]], writes=[qsqB], tag="qsq")
            for hb in range(2):
                sc.add("dve", lambda e, hb=hb, qsq=qsq: e.reduce_sum(
                    stat[hb][:, 0:16].unsqueeze(2),
                    qsq[:, hb * 2048:hb * 2048 + 2048].rearrange(
                        "p (g s) -> p g s", g=16), axis=AX.X),
                    reads=[qsqB], writes=[statB[hb]], tag="qss", fence=True)
                sc.add("dve", lambda e, hb=hb: e.tensor_scalar(
                    stat[hb][:, 0:16], stat[hb][:, 0:16], 1.0, HD * RMSEPS,
                    AL.mult, AL.add), reads=[statB[hb]], writes=[statB[hb]],
                    tag="qeps")
                sc.add("act", lambda e, hb=hb: e.activation(
                    stat[hb][:, 0:16], stat[hb][:, 0:16], AF.Sqrt),
                    reads=[statB[hb]], writes=[statB[hb]], tag="qsqrt")
                sc.add("dve", lambda e, hb=hb: e.reciprocal(
                    rstdq[:, hb * 16:hb * 16 + 16], stat[hb][:, 0:16]),
                    reads=[statB[hb]], writes=[rstdqB], tag="qrec", fence=True)
            for h in range(32):
                sc.add("dve", lambda e, h=h, t2=t2, Q_tm=Q_tm: e.tensor_scalar(
                    Q_tm[t2][:, h * 128:h * 128 + 128],
                    Q_tm[t2][:, h * 128:h * 128 + 128],
                    rstdq[:, h:h + 1], None, AL.mult),
                    reads=[rstdqB], writes=[Q_tmB[t2]], tag="qapp")
            dst = bass.AP(QhT, t2 * 128, [[32 * 256, 128], [256, 32], [1, 128]])
            dmaT(dst, Q_tm[t2][:], reads=[Q_tmB[t2]], writes=[QhTB], tag="qtr")
        close_phase(stR)
        stQ.close()

    def phase_attn_tile(half, qt, qs):
        gt = half * 4 + qt
        span = SPANS[gt]
        nck = span // 512
        ntk = span // 128
        stT = ExitStack()
        ssb = [sbin(stT, "ssb%d_%d" % (gt, i), [128, 2048], F32) for i in range(3)]
        ssbB = [_Buf() for _ in range(3)]
        Ab = [sbin(stT, "Ab%d_%d" % (gt, i), [128, 2048], BF16) for i in range(3)]
        AbB = [_Buf() for _ in range(3)]
        ATb = [sbin(stT, "ATb%d_%d" % (gt, i), [128, 16, 128], BF16,
                    is_transpose_dest=True) for i in range(3)]
        ATbB = [_Buf() for _ in range(3)]
        mkt = sbin(stT, "mk_%d" % gt, [128, 2048], F32)
        mktB = _Buf()
        rsum = [sbin(stT, "rsum%d_%d" % (gt, i), [128, 4], F32) for i in range(3)]
        rsumB = [_Buf() for _ in range(3)]
        dma("sp", mkt[:, 0:span], relmask[:, gt * 2048:gt * 2048 + span],
            reads=[], writes=[mktB], tag="mask")

        def scores_part(idx):
            kv, qh = idx // 4, idx % 4
            h = kv * 4 + qh
            pi = idx % 3
            slope = float(SLOPES[kv, qh])
            for ck in range(nck):
                bank = (idx * nck + ck) % 3
                sc.add("pe", lambda e, h=h, kv=kv, ck=ck, bank=bank: e.matmul(
                    pb[bank][:], QhT[:, h, qs * 128:qs * 128 + 128],
                    KhT[:, kv, ck * 512:ck * 512 + 512],
                    start=True, stop=True, skip_group_check=True),
                    reads=[QhTB, KhTB], writes=[pbB[bank]], tag="sc_mm")
                sc.add("dve", lambda e, pi=pi, ck=ck, slope=slope, bank=bank:
                       e.scalar_tensor_tensor(
                           ssb[pi][:, ck * 512:ck * 512 + 512],
                           mkt[:, ck * 512:ck * 512 + 512], slope,
                           pb[bank][:], AL.mult, AL.add),
                       reads=[mktB, pbB[bank]], writes=[ssbB[pi]], tag="sc_ep")
            sc.add("dve", lambda e, pi=pi: e.reduce_max(
                rsum[pi][:, 0:1].unsqueeze(2),
                ssb[pi][:, 0:span].unsqueeze(1), axis=AX.X, negate=True),
                reads=[ssbB[pi]], writes=[rsumB[pi]], tag="sc_max", fence=True)
            sc.add("act", lambda e, pi=pi: e.activation(
                Ab[pi][:, 0:span], ssb[pi][:, 0:span], AF.Exp,
                bias=rsum[pi][:, 0:1], accum_out=rsum[pi][:, 1:2]),
                reads=[ssbB[pi], rsumB[pi]], writes=[AbB[pi], rsumB[pi]],
                tag="sc_exp")
            sc.add("dve", lambda e, pi=pi: e.reciprocal(
                rsum[pi][:, 2:3], rsum[pi][:, 1:2]),
                reads=[rsumB[pi]], writes=[rsumB[pi]], tag="sc_rec", fence=True)
            sc.add("dve", lambda e, pi=pi: e.tensor_scalar(
                Ab[pi][:, 0:span], Ab[pi][:, 0:span],
                rsum[pi][:, 2:3], None, AL.mult),
                reads=[rsumB[pi]], writes=[AbB[pi]], tag="sc_nrm")
            dst = bass.AP(ATb[pi], 0, [[16 * 128, 128], [128, ntk], [1, 128]])
            dmaT(dst, Ab[pi][:, 0:span], reads=[AbB[pi]], writes=[ATbB[pi]],
                 tag="sc_T")

        def av_part(idx):
            kv, qh = idx // 4, idx % 4
            h = kv * 4 + qh
            pi = idx % 3
            avb = 4 + pi
            for tkc in range(ntk):
                sc.add("pe", lambda e, tkc=tkc, kv=kv, pi=pi, avb=avb: e.matmul(
                    pb[avb][:, 0:128],
                    V_tm[:, tkc, kv * 128:kv * 128 + 128],
                    ATb[pi][:, tkc, :], start=(tkc == 0),
                    stop=(tkc == ntk - 1), skip_group_check=True),
                    reads=[ATbB[pi], VB[tkc]],
                    writes=[pbB[avb]] if tkc in (0, ntk - 1) else [],
                    tag="av_mm")
            sc.add("act", lambda e, h=h, avb=avb: e.activation(
                outT[:, h, qt * 128:qt * 128 + 128], pb[avb][:, 0:128], AF.Copy),
                reads=[pbB[avb]], writes=[outTB], tag="av_ev")

        for idx in range(33):
            if idx < 32:
                scores_part(idx)
            if idx > 0:
                av_part(idx - 1)
        close_phase(stT)

    def phase_oproj(half):
        stOP = ExitStack()
        wsto = [sbin(stOP, "wsto%d_%d" % (half, i), [128, 32, 256], BF16)
                for i in range(2)]
        wstoB = [_Buf() for _ in range(2)]
        for mc in range(16):
            wi = mc % 2
            src = woT.rearrange("(c p) m -> p c m", p=128)[:, :, mc * 256:mc * 256 + 256]
            dma("pool", wsto[wi][:], src, reads=[], writes=[wstoB[wi]], tag="wo")
            for qt in range(4):
                gt = half * 4 + qt
                bank = 6 + (qt % 2)
                for h in range(32):
                    sc.add("pe", lambda e, h=h, qt=qt, wi=wi, bank=bank, wsto=wsto:
                           e.matmul(
                               pb[bank][:, 0:256], outT[:, h, qt * 128:qt * 128 + 128],
                               wsto[wi][:, h, :], start=(h == 0), stop=(h == 31),
                               skip_group_check=True),
                           reads=[outTB, wstoB[wi]],
                           writes=[pbB[bank]] if h in (0, 31) else [], tag="op_mm")
                xi = qt % 2
                dma("sp", x2p[xi][:, 0:256],
                    xown[gt * 128:gt * 128 + 128, mc * 256:mc * 256 + 256],
                    reads=[], writes=[x2pB[xi]], tag="op_x")
                sc.add("dve", lambda e, xi=xi, bank=bank: e.tensor_add(
                    x2p[xi][:, 256:512], pb[bank][:, 0:256], x2p[xi][:, 0:256]),
                    reads=[pbB[bank], x2pB[xi]], writes=[x2pB[xi]], tag="op_add")
                dma("sp", x2d[gt * 128:gt * 128 + 128, mc * 256:mc * 256 + 256],
                    x2p[xi][:, 256:512], reads=[x2pB[xi]], writes=[], tag="op_sp")
        close_phase(stOP)

    for half in range(2):
        for pr in range(2):
            phase_q_pair(half, pr)
            if dbg == "a1b" and half == 0 and pr == 0:
                dQ = dbg_out("dQhT", [128, 32 * 256], BF16)
                dma("sp", dQ[:, :], QhT[:].rearrange("p a b -> p (a b)"),
                    reads=[QhTB], writes=[], tag="dbgQ")
                return _emit(nc, sc, gstk, dbg_outs)
            for t2 in range(2):
                phase_attn_tile(half, pr * 2 + t2, t2)
        if dbg == "a2" and half == 0:
            dO = dbg_out("doutT", [128, 32 * 512], BF16)
            dma("sp", dO[:, :], outT[:].rearrange("p a b -> p (a b)"),
                reads=[outTB], writes=[], tag="dbgO")
            return _emit(nc, sc, gstk, dbg_outs)
        phase_oproj(half)
    close_phase(stAH)
    close_phase(stA)
    if dbg == "oproj":
        return _emit(nc, sc, gstk, dbg_outs)

    stLN = ExitStack()
    y2Te = sbin(stLN, "y2Te", [128, 16, 1024], BF16, is_transpose_dest=True)
    y2To = sbin(stLN, "y2To", [128, 16, 1024], BF16, is_transpose_dest=True)
    y2TeB, y2ToB = _Buf(), _Buf()
    Xg = sbin(stLN, "Xg", [128, 8, 128], BF16)
    XgB = [_Buf() for _ in range(8)]

    stL2 = ExitStack()
    xb2 = [sbin(stL2, "xb2_%d" % i, [128, 4096], F32) for i in range(2)]
    xb2B = [_Buf() for _ in range(2)]
    jk2 = sbin(stL2, "jk2", [128, 4096], BF16)
    jk2B = _Buf()
    y2e = [sbin(stL2, "y2e%d" % i, [128, 2048], BF16) for i in range(2)]
    y2o = [sbin(stL2, "y2o%d" % i, [128, 2048], BF16) for i in range(2)]
    y2eB = [_Buf() for _ in range(2)]
    y2oB = [_Buf() for _ in range(2)]
    xgs = sbin(stL2, "xgs", [128, 128], F32)
    xgsB = _Buf()

    for gt in range(8):
        i = gt % 2
        dma("sp", xb2[i][:], x2d[gt * 128:gt * 128 + 128, :], reads=[],
            writes=[xb2B[i]], tag="ln2_x")
        ln_stats(xb2[i], xb2B[i], jk2, jk2B, stat[i], statB[i], LNEPS, "ln2_%d" % gt)
        for par, yy, yyB in ((0, y2e[i], y2eB[i]), (1, y2o[i], y2oB[i])):
            sc.add("dve", lambda e, i=i, par=par, yy=yy: e.tensor_scalar(
                yy[:], xb2[i][:, par:4096:2], stat[i][:, 2:3], stat[i][:, 7:8],
                AL.subtract, AL.mult),
                reads=[xb2B[i], statB[i]], writes=[yyB], tag="ln2app")
        sc.add("dve", lambda e, i=i: e.reduce_sum(
            xgs[:, 0:64].unsqueeze(2),
            y2e[i][:].rearrange("p (g s) -> p g s", g=64), axis=AX.X),
            reads=[y2eB[i]], writes=[xgsB], tag="xg_e", fence=True)
        sc.add("dve", lambda e, i=i: e.reduce_sum(
            xgs[:, 64:128].unsqueeze(2),
            y2o[i][:].rearrange("p (g s) -> p g s", g=64), axis=AX.X),
            reads=[y2oB[i]], writes=[xgsB], tag="xg_o", fence=True)
        sc.add("dve", lambda e, gt=gt: e.tensor_add(
            Xg[:, gt, 0:64], xgs[:, 0:64], xgs[:, 64:128]),
            reads=[xgsB], writes=[XgB[gt]], tag="xg_add")
        sc.add("dve", lambda e, gt=gt: e.memset(Xg[:, gt, 64:128], 0.0),
               reads=[], writes=[XgB[gt]], tag="xg_pad")
        dstE = bass.AP(y2Te, gt * 128, [[16 * 1024, 128], [1024, 16], [1, 128]])
        dmaT(dstE, y2e[i][:], reads=[y2eB[i]], writes=[y2TeB], tag="y2eT")
        dstO = bass.AP(y2To, gt * 128, [[16 * 1024, 128], [1024, 16], [1, 128]])
        dmaT(dstO, y2o[i][:], reads=[y2oB[i]], writes=[y2ToB], tag="y2oT")
        dstX = bass.AP(XgT, gt * 128, [[8 * 128, 128], [128, 1], [1, 128]])
        dmaT(dstX, Xg[:, gt, :], reads=[XgB[gt]], writes=[XgTB], tag="xgT")

    if dbg == "ln2":
        dE = dbg_out("dy2Te", [128, 16 * 1024], BF16)
        dO2 = dbg_out("dy2To", [128, 16 * 1024], BF16)
        dX = dbg_out("dXgT", [128, 8 * 128], BF16)
        dma("sp", dE[:, :], y2Te[:].rearrange("p a b -> p (a b)"), reads=[y2TeB],
            writes=[], tag="d1")
        dma("sp", dO2[:, :], y2To[:].rearrange("p a b -> p (a b)"), reads=[y2ToB],
            writes=[], tag="d2")
        dma("sp", dX[:, :], XgT[:].rearrange("p a b -> p (a b)"), reads=[XgTB],
            writes=[], tag="d3")
        return _emit(nc, sc, gstk, dbg_outs)
    close_phase(stL2)

    stG = ExitStack()
    gsc_sb = sbin(stG, "gsc_sb", [64, INTER], BF16)
    usc_sb = sbin(stG, "usc_sb", [64, INTER], BF16)
    gzs_sb = sbin(stG, "gzs_sb", [64, INTER], BF16)
    uzs_sb = sbin(stG, "uzs_sb", [64, INTER], BF16)
    gscB, uscB, gzsB, uzsB = _Buf(), _Buf(), _Buf(), _Buf()
    stripg_sb = sbin(stG, "stripg_sb", [64, 2048], BF16)
    stripgB = _Buf()
    pk = [sbin(stG, "pk%d" % i, [128, 512], U8) for i in range(4)]
    pkB = [_Buf() for _ in range(4)]
    dq = [sbin(stG, "dq%d" % i, [128, 256], BF16) for i in range(4)]
    dqB = [_Buf() for _ in range(4)]
    silu_st = sbin(stG, "silu_st", [128, 2, 1024], BF16)
    silu_stB = _Buf()
    hstage = sbin(stG, "hstage", [128, 2, 1024], BF16)
    hstageB = _Buf()

    dma("pool", gsc_sb[:], gsc[:, :], reads=[], writes=[gscB], tag="gsc")
    dma("pool", usc_sb[:], usc[:, :], reads=[], writes=[uscB], tag="usc")
    dma("sp", stripg_sb[:], stripg[0:64, 0:2048], reads=[], writes=[stripgB],
        tag="strip")
    for zsrc, ssb_, sB_, zdst, zB in ((gzr, gsc_sb, gscB, gzs_sb, gzsB),
                                      (uzr, usc_sb, uscB, uzs_sb, uzsB)):
        dma("pool", zdst[:], zsrc[:, :], reads=[], writes=[zB], tag="zr")
        sc.add("dve", lambda e, s=ssb_, z=zdst: e.tensor_mul(z[:], z[:], s[:]),
               reads=[sB_, zB], writes=[zB], tag="zs_m")
        sc.add("dve", lambda e, z=zdst: e.tensor_scalar(
            z[:], z[:], -1.0, None, AL.mult), reads=[zB], writes=[zB], tag="zs_n")

    def xgt_rhs(th):
        return bass.AP(XgT, th * 4 * 128, [[8 * 128, 64], [128, 4], [1, 128]])

    def gu_pk_load(pack, i0, wc):
        pp = wc % 4
        dma("sp", pk[pp][:, 0:256], pack[wc * 128:wc * 128 + 128, i0:i0 + 256],
            reads=[], writes=[pkB[pp]], tag="gu_pk")

    def gu_bc(scl, sclB, i0, wc):
        bb = 4 + wc % 2
        sc.add("pe", lambda e, wc=wc, scl=scl, i0=i0, bb=bb: e.matmul(
            pb[bb][:, 0:256], stripg_sb[:, wc * 128:wc * 128 + 128],
            scl[:, i0:i0 + 256], start=True, stop=True,
            skip_group_check=True),
            reads=[stripgB, sclB], writes=[pbB[bb]], tag="gu_bc")

    for ib in range(43):
        i0 = ib * 256
        for mat in range(2):
            pack = gpack if mat == 0 else upack
            scl, sclB = (gsc_sb, gscB) if mat == 0 else (usc_sb, uscB)
            zs, zsB = (gzs_sb, gzsB) if mat == 0 else (uzs_sb, uzsB)
            for it in range(2):
                for th in range(2):
                    bank = it * 2 + th
                    sc.add("pe", lambda e, bank=bank, zs=zs, i0=i0, it=it, th=th:
                           e.matmul(pb[bank][:],
                                    zs[:, i0 + it * 128:i0 + it * 128 + 128],
                                    xgt_rhs(th), start=True, stop=False,
                                    skip_group_check=True),
                           reads=[zsB, XgTB], writes=[pbB[bank]], tag="gu_corr")
            gu_pk_load(pack, i0, 0)
            gu_pk_load(pack, i0, 1)
            gu_bc(scl, sclB, i0, 0)
            gu_bc(scl, sclB, i0, 1)
            for wc in range(16):
                pp = wc % 4
                bb = 4 + wc % 2
                hi_t, hiB = dq[0 + wc % 2], dqB[0 + wc % 2]
                lo_t, loB = dq[2 + wc % 2], dqB[2 + wc % 2]
                sc.add("dve", lambda e, pp=pp: e.tensor_scalar(
                    pk[pp][:, 256:512].bitcast(U32), pk[pp][:, 0:256].bitcast(U32),
                    4, 0x0F0F0F0F, AL.logical_shift_right, AL.bitwise_and),
                    reads=[pkB[pp]], writes=[pkB[pp]], tag="gu_hi")
                sc.add("dve", lambda e, pp=pp: e.tensor_scalar(
                    pk[pp][:, 0:256].bitcast(U32), pk[pp][:, 0:256].bitcast(U32),
                    0x0F0F0F0F, None, AL.bitwise_and),
                    reads=[pkB[pp]], writes=[pkB[pp]], tag="gu_lo")
                sc.add("dve", lambda e, pp=pp, hi_t=hi_t, bb=bb: e.scalar_tensor_tensor(
                    hi_t[:], pk[pp][:, 256:512], 1, pb[bb][:, 0:256],
                    AL.mult, AL.mult),
                    reads=[pkB[pp], pbB[bb]], writes=[hiB], tag="gu_dqh")
                sc.add("dve", lambda e, pp=pp, lo_t=lo_t, bb=bb: e.scalar_tensor_tensor(
                    lo_t[:], pk[pp][:, 0:256], 1, pb[bb][:, 0:256],
                    AL.mult, AL.mult),
                    reads=[pkB[pp], pbB[bb]], writes=[loB], tag="gu_dql")
                if wc + 2 < 16:
                    gu_pk_load(pack, i0, wc + 2)
                for nib in range(2):
                    wti, wtiB = (hi_t, hiB) if nib == 0 else (lo_t, loB)
                    ysrc, ysrcB = (y2Te, y2TeB) if nib == 0 else (y2To, y2ToB)
                    for it in range(2):
                        for th in range(2):
                            bank = it * 2 + th
                            last = (wc == 15 and nib == 1)
                            sc.add("pe", lambda e, wti=wti, it=it, th=th, wc=wc,
                                   ysrc=ysrc, bank=bank, last=last: e.matmul(
                                       pb[bank][:], wti[:, it * 128:it * 128 + 128],
                                       ysrc[:, wc, th * 512:th * 512 + 512],
                                       start=False, stop=last, skip_group_check=True),
                                   reads=[wtiB, ysrcB],
                                   writes=[pbB[bank]] if last else [], tag="gu_mm")
                if wc + 2 < 16:
                    gu_bc(scl, sclB, i0, wc + 2)
            for it in range(2):
                for th in range(2):
                    bank = it * 2 + th
                    if mat == 0:
                        sc.add("act", lambda e, it=it, th=th, bank=bank: e.activation(
                            silu_st[:, it, th * 512:th * 512 + 512], pb[bank][:],
                            AF.Silu),
                            reads=[pbB[bank]], writes=[silu_stB], tag="gu_silu")
                    else:
                        sc.add("dve", lambda e, it=it, th=th, bank=bank: e.tensor_mul(
                            hstage[:, it, th * 512:th * 512 + 512],
                            silu_st[:, it, th * 512:th * 512 + 512], pb[bank][:]),
                            reads=[silu_stB, pbB[bank]], writes=[hstageB],
                            tag="gu_hmul")
        for it in range(2):
            dma("sp", had[i0 + it * 128:i0 + it * 128 + 128, :],
                hstage[:, it, :], reads=[hstageB], writes=[], tag="gu_spill")

    if dbg == "gate":
        return _emit(nc, sc, gstk, dbg_outs)
    close_phase(stG)
    close_phase(stLN)

    stD = ExitStack()
    hact = sbin(stD, "hact", [128, 86, 512], BF16)
    hactB = [_Buf() for _ in range(86)]
    strip_sb = sbin(stD, "strip_sb", [128, 5504], BF16)
    stripB = _Buf()
    zsd = sbin(stD, "zsd", [128, 3, D], BF16)
    zsdB = _Buf()
    tmpd = sbin(stD, "tmpd", [128, 3 * D], BF16)
    tmpdB = _Buf()
    dscA = sbin(stD, "dscA", [128, D], BF16)
    dscAB = _Buf()
    dscC = sbin(stD, "dscC", [64, D], BF16)
    dscCB = _Buf()
    st252 = sbin(stD, "st252", [128, 256], BF16)
    st252B = _Buf()
    XgT2 = sbin(stD, "XgT2", [128, 3, 512], BF16)
    XgT2B = _Buf()
    dpk = [sbin(stD, "dpk%d" % i, [128, 1024], U8) for i in range(4)]
    dpkB = [_Buf() for _ in range(4)]
    dqd = [sbin(stD, "dqd%d" % i, [128, 512], BF16) for i in range(4)]
    dqdB = [_Buf() for _ in range(4)]
    sclst = [sbin(stD, "sclst%d" % i, [128, 512], BF16) for i in range(2)]
    sclstB = [_Buf() for _ in range(2)]
    outp = [sbin(stD, "outp%d" % i, [128, 512], F32) for i in range(2)]
    outpB = [_Buf() for _ in range(2)]
    b4tok = [_Buf(), _Buf()]

    dma("pool", dscA[:], dsc0[:, :], reads=[], writes=[dscAB], tag="dscA")
    dma("pool", dscC[:], dsc1[:, :], reads=[], writes=[dscCB], tag="dscC")
    dma("sp", st252[:], strip252[:, :], reads=[], writes=[st252B], tag="st252")
    dma("sp", strip_sb[:], stripg[:, :], reads=[], writes=[stripB], tag="stripd")
    dma("pool", zsd[:].rearrange("p a b -> p (a b)"), d2s[:, :], reads=[],
        writes=[zsdB], tag="d2s")
    dma("pool", tmpd[:], d2z[:, :], reads=[], writes=[tmpdB], tag="d2z")
    sc.add("dve", lambda e: e.tensor_mul(
        zsd[:].rearrange("p a b -> p (a b)"), zsd[:].rearrange("p a b -> p (a b)"),
        tmpd[:]), reads=[tmpdB, zsdB], writes=[zsdB], tag="zsd_m")
    sc.add("dve", lambda e: e.tensor_scalar(
        zsd[:].rearrange("p a b -> p (a b)"), zsd[:].rearrange("p a b -> p (a b)"),
        -1.0, None, AL.mult), reads=[zsdB], writes=[zsdB], tag="zsd_n")

    n_th = 1 if dbg in ("down0", "down1") else 2
    n_hc = 1 if dbg == "down1" else (0 if dbg == "down0" else 8)
    for th in range(n_th):
        for ic in range(86):
            dma("sp", hact[:, ic, :],
                had[ic * 128:ic * 128 + 128, th * 512:th * 512 + 512],
                reads=[], writes=[hactB[ic]], tag="dn_h")
        for ic in range(86):
            bk = 5 + ic // 32
            cp = ic % 32
            first = cp == 0
            last = (ic % 32 == 31) or (ic == 85)
            sc.add("pe", lambda e, ic=ic, bk=bk, cp=cp, first=first, last=last:
                   e.matmul(pb[bk][:], st252[:, 124 - 4 * cp:252 - 4 * cp],
                            hact[:, ic, :], start=first, stop=last,
                            skip_group_check=True),
                   reads=[st252B, hactB[ic]],
                   writes=[pbB[bk]] if (first or last) else [], tag="dn_xg")
        for bk2 in range(3):
            sc.add("dve", lambda e, bk2=bk2: e.tensor_copy(
                XgT2[:, bk2, :], pb[5 + bk2][:]),
                reads=[pbB[5 + bk2]], writes=[XgT2B], tag="dn_xge")
        for hc in range(n_hc):
            def dn_pk_load(wc, hc=hc):
                pp = wc % 4
                dma("sp", dpk[pp][:, 0:512],
                    dpack[wc * 128:wc * 128 + 128, hc * 512:hc * 512 + 512],
                    reads=[], writes=[dpkB[pp]], tag="dn_pk")

            def dn_bc(wc, hc=hc):
                bb = 4 + wc % 2
                if wc < 32:
                    sc.add("pe", lambda e, wc=wc, bb=bb: e.matmul(
                        pb[bb][:], strip_sb[0:128, wc * 128:wc * 128 + 128],
                        dscA[:, hc * 512:hc * 512 + 512], start=True, stop=True,
                        skip_group_check=True),
                        reads=[stripB, dscAB], writes=[pbB[bb]], tag="dn_bc")
                else:
                    sc.add("pe", lambda e, wc=wc, bb=bb: e.matmul(
                        pb[bb][:], strip_sb[0:64, (wc - 32) * 128:(wc - 32) * 128 + 128],
                        dscC[:, hc * 512:hc * 512 + 512], start=True, stop=True,
                        skip_group_check=True),
                        reads=[stripB, dscCB], writes=[pbB[bb]], tag="dn_bc")

            dn_pk_load(0)
            dn_pk_load(1)
            dn_bc(0)
            dn_bc(1)
            for wc in range(43):
                pp = wc % 4
                bb = 4 + wc % 2
                b4t = b4tok[wc % 2]
                sc.add("act", lambda e, pp=pp, bb=bb: e.activation(
                    sclst[pp % 2][:], pb[bb][:], AF.Copy),
                    reads=[pbB[bb]], writes=[sclstB[pp % 2], b4t], tag="dn_bcev")
                sc.add("dve", lambda e, pp=pp: e.tensor_scalar(
                    dpk[pp][:, 512:1024].bitcast(U32), dpk[pp][:, 0:512].bitcast(U32),
                    4, 0x0F0F0F0F, AL.logical_shift_right, AL.bitwise_and),
                    reads=[dpkB[pp]], writes=[dpkB[pp]], tag="dn_hi")
                sc.add("dve", lambda e, pp=pp: e.tensor_scalar(
                    dpk[pp][:, 0:512].bitcast(U32), dpk[pp][:, 0:512].bitcast(U32),
                    0x0F0F0F0F, None, AL.bitwise_and),
                    reads=[dpkB[pp]], writes=[dpkB[pp]], tag="dn_lo")
                hi_t, hiB = dqd[0 + wc % 2], dqdB[0 + wc % 2]
                lo_t, loB = dqd[2 + wc % 2], dqdB[2 + wc % 2]
                sc.add("dve", lambda e, pp=pp, hi_t=hi_t, bb=bb: e.scalar_tensor_tensor(
                    hi_t[:], dpk[pp][:, 512:1024], 1, pb[bb][:], AL.mult, AL.mult),
                    reads=[dpkB[pp], pbB[bb], b4t], writes=[hiB], tag="dn_dqh")
                sc.add("pool", lambda e, pp=pp, lo_t=lo_t: e.tensor_mul(
                    lo_t[:], dpk[pp][:, 0:512], sclst[pp % 2][:]),
                    reads=[dpkB[pp], sclstB[pp % 2]], writes=[loB], tag="dn_dql")
                if wc + 2 < 43:
                    dn_pk_load(wc + 2)
                for nib in range(2):
                    wti, wtiB = (hi_t, hiB) if nib == 0 else (lo_t, loB)
                    icc = wc if nib == 0 else 43 + wc
                    for ts in range(4):
                        first = (wc == 0 and nib == 0)
                        sc.add("pe", lambda e, wti=wti, icc=icc, ts=ts, first=first:
                               e.matmul(pb[ts][:],
                                        hact[:, icc, ts * 128:ts * 128 + 128],
                                        wti[:], start=first, stop=False,
                                        skip_group_check=True),
                               reads=[wtiB, hactB[icc]],
                               writes=[pbB[ts]] if first else [], tag="dn_mm")
                if wc + 2 < 43:
                    dn_bc(wc + 2)
            for ts in range(4):
                for b2 in range(3):
                    last = b2 == 2
                    sc.add("pe", lambda e, ts=ts, b2=b2, hc=hc, last=last: e.matmul(
                        pb[ts][:], XgT2[:, b2, ts * 128:ts * 128 + 128],
                        zsd[:, b2, hc * 512:hc * 512 + 512],
                        start=False, stop=last, skip_group_check=True),
                        reads=[XgT2B, zsdB],
                        writes=[pbB[ts]] if last else [], tag="dn_corr")
                oi = ts % 2
                r0 = th * 512 + ts * 128
                dma("sp", x2p[oi][:, 0:512],
                    x2d[r0:r0 + 128, hc * 512:hc * 512 + 512],
                    reads=[], writes=[x2pB[oi]], tag="dn_x2")
                sc.add("dve", lambda e, oi=oi, ts=ts: e.tensor_add(
                    outp[oi][:], pb[ts][:], x2p[oi][:, 0:512]),
                    reads=[pbB[ts], x2pB[oi]], writes=[outpB[oi]], tag="dn_add")
                dma("sp", out[r0:r0 + 128, hc * 512:hc * 512 + 512], outp[oi][:],
                    reads=[outpB[oi]], writes=[], tag="dn_out")

    if dbg in ("down0", "down1"):
        dX2 = dbg_out("dXgT2", [128, 3 * 512], BF16)
        dma("sp", dX2[:, :], XgT2[:].rearrange("p a b -> p (a b)"), reads=[XgT2B],
            writes=[], tag="dx2")
    close_phase(stD)
    return _emit(nc, sc, gstk, dbg_outs)


def _emit(nc, sc, gstk, dbg_outs):
    sc.barrier()
    sc.add("sp", lambda e: e.nop(), reads=[], writes=[], tag="final")
    sc.finalize(nc, gstk)
    with nc.Block() as block:
        @block.tensor
        def _(e):
            sc.emit("pe", e)

        @block.vector
        def _(e):
            sc.emit("dve", e)

        @block.scalar
        def _(e):
            sc.emit("act", e)

        @block.gpsimd
        def _(e):
            sc.emit("pool", e)

        @block.sync
        def _(e):
            sc.emit("sp", e)
    return nc, dbg_outs


def kernel(**inputs):
    sh, cores = _host_prep(inputs)
    nc, dbg_outs = _build(_DBG)
    in_maps = []
    for pc, b, chunks in cores:
        m = dict(sh)
        m.update(pc)
        in_maps.append(m)
    res = run_bass_kernel_spmd(nc, in_maps, core_ids=list(range(8)))
    out = np.zeros((4, 2048, 4096), dtype=np.float32)
    for c, (pc, b, chunks) in enumerate(cores):
        oc = np.asarray(res.results[c]["out"])
        for s_, ch in enumerate(chunks):
            out[b, ch * 256:(ch + 1) * 256] = oc[s_ * 256:(s_ + 1) * 256]
    return out
